# version 09
# baseline (speedup 1.0000x reference)
import numpy as np

import bass_rust as _bass_rust
import concourse.bass as bass
import concourse.mybir as mybir
from concourse.bass_isa import ReduceOp
from concourse.bass_utils import run_bass_kernel_spmd
from concourse.library_config import all_libraries, standard

F32 = mybir.dt.float32
F16 = mybir.dt.float16
Alu = mybir.AluOpType
Act = mybir.ActivationFunctionType
AX = mybir.AxisListType

C32 = 12582912.0
C64 = 6755399441055744.0

NT = 16
TT = 256
D, DI, DF = 1024, 2048, 4096
DC, CC, FC = 8, 16, 32

ENGINES = ("tensor", "vector", "scalar", "gpsimd", "sync")


class B:

    def __init__(self):
        self.ops = {e: [] for e in ENGINES}
        self.ecount = {e: 0 for e in ENGINES}
        self.dcount = {}
        self.ver = {}
        self.readers = {}
        self.last_wait = {e: {} for e in ENGINES}
        self.dirty_w = {e: set() for e in ENGINES}
        self.dirty_r = {e: set() for e in ENGINES}

    def _collect(self, engine, reads, writes):
        waits = {}
        drain = False
        no_drain = engine == "tensor"

        def cum(sk, val):
            if sk[0] == "dma":
                return self.dcount.get(sk[1], 0)
            return val

        for b in reads:
            v = self.ver.get(b)
            if v is not None:
                sk, val = v
                if sk == ("eng", engine):
                    if b in self.dirty_w[engine] and not no_drain:
                        drain = True
                else:
                    val = cum(sk, val)
                    if waits.get(sk, -1) < val:
                        waits[sk] = val
        for b in writes:
            v = self.ver.get(b)
            if v is not None:
                sk, val = v
                if sk != ("eng", engine):
                    val = cum(sk, val)
                    if waits.get(sk, -1) < val:
                        waits[sk] = val
            for sk, val in self.readers.get(b, {}).items():
                if sk == ("eng", engine):
                    if engine == "gpsimd" and b in self.dirty_r[engine]:
                        drain = True
                else:
                    val = cum(sk, val)
                    if waits.get(sk, -1) < val:
                        waits[sk] = val
        wlist = []
        for sk, val in waits.items():
            if self.last_wait[engine].get(sk, -1) < val:
                wlist.append((sk, val))
                self.last_wait[engine][sk] = val
        return wlist, drain

    def op(self, engine, fn, reads=(), writes=()):
        wlist, drain = self._collect(engine, reads, writes)
        if drain:
            self.dirty_w[engine] = set()
            self.dirty_r[engine] = set()
        self.ecount[engine] += 1
        myval = self.ecount[engine]
        me = ("eng", engine)
        for b in writes:
            self.ver[b] = (me, myval)
            self.readers[b] = {}
            self.dirty_w[engine].add(b)
        for b in reads:
            self.readers.setdefault(b, {})[me] = myval
            self.dirty_r[engine].add(b)
        self.ops[engine].append(("op", drain, wlist, fn))

    def dma(self, engine, stream, fn, reads=(), writes=()):
        wlist, drain = self._collect(engine, reads, writes)
        if drain:
            self.dirty_w[engine] = set()
            self.dirty_r[engine] = set()
        cnt = self.dcount.get(stream, 0) + 16
        self.dcount[stream] = cnt
        sk = ("dma", stream)
        for b in writes:
            self.ver[b] = (sk, cnt)
            self.readers[b] = {}
        for b in reads:
            rd = self.readers.setdefault(b, {})
            if rd.get(sk, -1) < cnt:
                rd[sk] = cnt
        self.ops[engine].append(("dma", drain, wlist, fn, stream))

    def final_wait(self, engine, stream):
        self.ops[engine].append(
            ("wait", False, [(("dma", stream), self.dcount.get(stream, 0))], None)
        )


def _emit(nc, plan, esems, dsems):
    with nc.Block() as block:
        def section(ename):
            def run(eng):
                for item in plan.ops[ename]:
                    kind = item[0]
                    drain, wlist = item[1], item[2]
                    if drain:
                        eng.drain()
                    for sk, val in wlist:
                        sem = esems[sk[1]] if sk[0] == "eng" else dsems[sk[1]]
                        eng.wait_ge(sem, val)
                    if kind == "op":
                        ins = item[3](eng)
                        ins.then_inc(esems[ename], 1)
                    elif kind == "dma":
                        ins = item[3](eng)
                        ins.then_inc(dsems[item[4]], 16)
            return run

        block.tensor(section("tensor"))
        block.vector(section("vector"))
        block.scalar(section("scalar"))
        block.gpsimd(section("gpsimd"))
        block.sync(section("sync"))


def build_program(n_tiles=NT, debug=False):
    nc = bass.Bass("TRN2", target_bir_lowering=False,
                   detect_race_conditions=False)

    x_h = nc.dram_tensor("x", [2, NT, 128, DC, 128], F32, kind="ExternalInput")
    o_h = nc.dram_tensor("out", [2, NT, 128, DC, 128], F32, kind="ExternalOutput")
    win_h = nc.dram_tensor("w_in", [DC, 128, 2 * DI], F32, kind="ExternalInput")
    wd_h = nc.dram_tensor("w_xd", [128, CC], F32, kind="ExternalInput")
    wbc_h = nc.dram_tensor("w_xbc", [128, CC, 128], F32, kind="ExternalInput")
    wB_h = nc.dram_tensor("w_B", [64, CC, 128], F32, kind="ExternalInput")
    wC_h = nc.dram_tensor("w_C", [64, CC, 128], F32, kind="ExternalInput")
    wo_h = nc.dram_tensor("w_out", [CC, 128, D], F32, kind="ExternalInput")
    dD_h = nc.dram_tensor("d_vec", [128, CC], F32, kind="ExternalInput")
    al_h = nc.dram_tensor("a_log", [1, 1], F32, kind="ExternalInput")
    w1_h = nc.dram_tensor("w1t", [DC, 128, DF], F32, kind="ExternalInput")
    b1_h = nc.dram_tensor("b1c", [128, FC], F32, kind="ExternalInput")
    w2_h = nc.dram_tensor("w2t", [FC, 128, D], F32, kind="ExternalInput")
    b2_h = nc.dram_tensor("b2c", [128, DC], F32, kind="ExternalInput")
    wq1_h = nc.dram_tensor("wq1d", [DC, 128, DF], F16, kind="Internal")
    wq2_h = nc.dram_tensor("wq2d", [FC, 128, D], F16, kind="Internal")

    dbg = {}
    if debug:
        for nm, sh in [("d_x1", [128, DC, TT]), ("d_xi", [128, CC, TT]),
                       ("d_s", [128, CC, TT]), ("d_h", [128, CC, TT]),
                       ("d_y2", [128, CC, TT]), ("d_u", [128, DC, TT]),
                       ("d_q1", [128, DC, TT]), ("d_g", [128, FC, TT]),
                       ("d_abar", [1, TT]), ("d_mix", [128, DC, TT])]:
            dbg[nm] = nc.dram_tensor(nm, sh, F32, kind="ExternalOutput")

    from contextlib import ExitStack
    stack = ExitStack()
    e = stack.enter_context
    with stack:
        x_sb = e(nc.sbuf_tensor("x_sb", [128, DC, TT], F32))
        x1_sb = e(nc.sbuf_tensor("x1_sb", [128, DC, TT], F32))
        sq_sb = e(nc.sbuf_tensor("sq_sb", [128, DC, TT], F32))
        st_sb = e(nc.sbuf_tensor("st_sb", [128, DC, TT], F32))
        mx_sb = e(nc.sbuf_tensor("mx_sb", [128, DC, TT], F32))
        xz_sb = e(nc.sbuf_tensor("xz_sb", [128, 2 * CC, TT], F32))
        bb_sb = e(nc.sbuf_tensor("bb_sb", [128, CC, TT], F32))
        ot_sb = e(nc.sbuf_tensor("ot_sb", [128, DC, TT], F32))
        hh_sb = e(nc.sbuf_tensor("hh_sb", [128, CC, TT], F32))
        q1_sb = e(nc.sbuf_tensor("q1_sb", [128, DC, TT], F16))
        q2_sb = e(nc.sbuf_tensor("q2_sb", [128, FC, TT], F16))
        carry = e(nc.sbuf_tensor("carry", [128, 32], F32))
        braw = e(nc.sbuf_tensor("braw", [64, TT], F32))
        craw = e(nc.sbuf_tensor("craw", [64, TT], F32))
        abc_sb = e(nc.sbuf_tensor("abc_sb", [128, TT], F32))
        t_mm = e(nc.sbuf_tensor("t_mm", [128, TT], F32))
        t_rr = e(nc.sbuf_tensor("t_rr", [128, TT], F32))
        t_ms = e(nc.sbuf_tensor("t_ms", [128, TT], F32))
        t_ax = e(nc.sbuf_tensor("t_ax", [128, TT], F32))
        t_sc = e(nc.sbuf_tensor("t_sc", [128, TT], F32))
        t_fs = e(nc.sbuf_tensor("t_fs", [128, TT], F32))
        t_aq = e(nc.sbuf_tensor("t_aq", [128, TT], F32))
        t_am = e(nc.sbuf_tensor("t_am", [128, TT], F32))
        t_rc = e(nc.sbuf_tensor("t_rc", [128, TT], F32))
        r_ab = e(nc.sbuf_tensor("r_ab", [1, TT], F32))
        r_en = e(nc.sbuf_tensor("r_en", [1, TT], F32))
        r_ln = e(nc.sbuf_tensor("r_ln", [1, TT], F32))
        r_mx = e(nc.sbuf_tensor("r_mx", [1, TT], F32))
        r_dl = e(nc.sbuf_tensor("r_dl", [1, TT], F32))
        r_ar = e(nc.sbuf_tensor("r_ar", [1, TT], F32))
        wd_sb = e(nc.sbuf_tensor("wd_sb", [128, CC], F32))
        wbc_sb = e(nc.sbuf_tensor("wbc_sb", [128, CC, 128], F32))
        wB_sb = e(nc.sbuf_tensor("wB_sb", [64, CC, 128], F32))
        wC_sb = e(nc.sbuf_tensor("wC_sb", [64, CC, 128], F32))
        dD_sb = e(nc.sbuf_tensor("dD_sb", [128, CC], F32))
        b1_sb = e(nc.sbuf_tensor("b1_sb", [128, FC], F32))
        b2_sb = e(nc.sbuf_tensor("b2_sb", [128, DC], F32))
        al_sb = e(nc.sbuf_tensor("al_sb", [1, 1], F32))
        an_sb = e(nc.sbuf_tensor("an_sb", [1, 1], F32))
        fF = e(nc.sbuf_tensor("fF", [128, 1], F32))
        eps_sb = e(nc.sbuf_tensor("eps_sb", [128, 1], F32))
        on_sb = e(nc.sbuf_tensor("on_sb", [128, 1], F32))
        r_sm = e(nc.sbuf_tensor("r_sm", [1, TT], F32))
        r_tmp = e(nc.sbuf_tensor("r_tmp", [1, TT], F32))
        r_rs = e(nc.sbuf_tensor("r_rs", [1, TT], F32))
        m1c = e(nc.sbuf_tensor("m1c", [128, 1], F32))
        m2c = e(nc.sbuf_tensor("m2c", [128, 1], F32))
        sw1 = e(nc.sbuf_tensor("sw1", [128, 1], F32))
        sw2 = e(nc.sbuf_tensor("sw2", [128, 1], F32))
        macc = e(nc.sbuf_tensor("macc", [128, 48], F32))
        winb = [e(nc.sbuf_tensor(f"winb{i}", [128, 1024], F32)) for i in (0, 1)]
        wob = [e(nc.sbuf_tensor(f"wob{i}", [128, 512], F32)) for i in (0, 1)]
        wq1b = [e(nc.sbuf_tensor(f"wq1b{i}", [128, 1024], F16)) for i in (0, 1)]
        wq2b = [e(nc.sbuf_tensor(f"wq2b{i}", [128, 1024], F16)) for i in (0, 1)]
        pm = [e(nc.psum_tensor(f"pm{i}", [128, 512], F32)) for i in range(4)]
        prow = e(nc.psum_tensor("prow", [1, 2 * TT], F32))
        pbc = e(nc.psum_tensor("pbc", [64, 512], F32))
        pB = e(nc.psum_tensor("pB", [128, 512], F32))
        pC = e(nc.psum_tensor("pC", [128, 512], F32))
        esems = {en: e(nc.semaphore(f"se_{en}")) for en in ENGINES}
        dstreams = (["x0", "x1", "out0", "out1", "xzld", "wqs0", "wqs1"]
                    + ["winb0", "winb1", "wob0", "wob1", "wq1b0", "wq1b1",
                       "wq2b0", "wq2b1"]
                    + ["r_wd", "r_wbc", "r_wB", "r_wC", "r_dD", "r_b1c",
                       "r_b2c", "r_alog"])
        dsems = {s: e(nc.semaphore(f"sd_{s}")) for s in dstreams}

        p = B()

        def bc(tile_ap, nch):
            return tile_ap[:, :].unsqueeze(1).broadcast_to([128, nch, TT])

        for (dst, src, bufn, strm) in [
            (wd_sb, wd_h, "wd", "r_wd"), (wbc_sb, wbc_h, "wbc", "r_wbc"),
            (wB_sb, wB_h, "wB", "r_wB"), (wC_sb, wC_h, "wC", "r_wC"),
            (dD_sb, dD_h, "dD", "r_dD"), (b1_sb, b1_h, "b1c", "r_b1c"),
            (b2_sb, b2_h, "b2c", "r_b2c"), (al_sb, al_h, "alog", "r_alog"),
        ]:
            p.dma("sync", strm,
                  (lambda d, s: lambda eng: eng.dma_start(d[...], s[...]))(dst, src),
                  writes=(bufn,))
        p.op("vector", lambda eng: eng.memset(fF[:, :], 1.0), writes=("fF",))
        p.op("vector",
             lambda eng: eng.tensor_scalar(fF[:, :], fF[:, :], C64, -C64, Alu.add, Alu.add),
             reads=("fF",), writes=("fF",))
        p.op("scalar",
             lambda eng: eng.activation(an_sb[:, :], al_sb[:, :], Act.Exp),
             reads=("alog",), writes=("aneg",))
        p.op("vector",
             lambda eng: eng.tensor_scalar(an_sb[:, :], an_sb[:, :], -1.0, None, Alu.mult),
             reads=("aneg",), writes=("aneg",))
        p.op("vector", lambda eng: eng.memset(carry[:, :], 0.0), writes=("carry",))
        p.op("vector", lambda eng: eng.memset(eps_sb[:, :], 1e-5), writes=("eps",))
        p.op("vector", lambda eng: eng.memset(on_sb[:, :], 1.0), writes=("ones",))

        xzf = xz_sb[:, :, :].rearrange("p c t -> p (c t)")
        q2f = q2_sb[:, :, :].rearrange("p c t -> p (c t)")

        def wmean(w_h, nch, wcols, mcol0, mname):
            for ch in range(nch):
                half = ch % 2
                p.dma("sync", "xzld",
                      (lambda hf, c: lambda eng: eng.dma_start(
                          xzf[:, hf * 4096:hf * 4096 + wcols], w_h[c]))(half, ch),
                      writes=("xz",))
                p.op("vector",
                     (lambda hf, c: lambda eng: eng.tensor_reduce(
                         macc[:, mcol0 + c:mcol0 + c + 1],
                         xzf[:, hf * 4096:hf * 4096 + wcols],
                         AX.X, Alu.max if False else Alu.add,
                         apply_absolute_value=True))(half, ch),
                     reads=("xz",), writes=(mname,))
            p.op("vector",
                 lambda eng: eng.tensor_reduce(
                     macc[:, mcol0:mcol0 + 1], macc[:, mcol0:mcol0 + nch],
                     AX.X, Alu.add),
                 reads=(mname,), writes=(mname,))
            p.op("gpsimd",
                 lambda eng: eng.partition_all_reduce(
                     macc[:, mcol0:mcol0 + 1], macc[:, mcol0:mcol0 + 1],
                     128, ReduceOp.add),
                 reads=(mname,), writes=(mname,))

        wmean(w1_h, DC, DF, 0, "macc1")
        wmean(w2_h, FC, D, 8, "macc2")
        p.op("vector",
             lambda eng: eng.tensor_scalar(m1c[:, :], macc[:, 0:1],
                                           1.0 / (DF * D), 1e-5, Alu.mult, Alu.max),
             reads=("macc1",), writes=("m1c",))
        p.op("vector", lambda eng: eng.reciprocal(sw1[:, :], m1c[:, :]),
             reads=("m1c",), writes=("sw1",))
        p.op("vector",
             lambda eng: eng.tensor_scalar(m2c[:, :], macc[:, 8:9],
                                           1.0 / (DF * D), 1e-5, Alu.mult, Alu.max),
             reads=("macc2",), writes=("m2c",))
        p.op("vector", lambda eng: eng.reciprocal(sw2[:, :], m2c[:, :]),
             reads=("m2c",), writes=("sw2",))

        sqf = sq_sb[:, :, :].rearrange("p c t -> p (c t)")
        stf = st_sb[:, :, :].rearrange("p c t -> p (c t)")
        mxf = mx_sb[:, :, :].rearrange("p c t -> p (c t)")

        def wquant(w_h, wq_hh, nch, wcols, swt, swn):
            npieces = (wcols + 2047) // 2048
            for ch in range(nch):
                half = ch % 2
                xsl = xzf[:, half * 4096: half * 4096 + wcols]
                qsl = q2f[:, half * 4096: half * 4096 + wcols]
                p.dma("sync", "xzld",
                      (lambda xs, c: lambda eng: eng.dma_start(xs, w_h[c]))(xsl, ch),
                      writes=("xz",))
                for pc in range(npieces):
                    c0 = pc * 2048
                    cw = min(2048, wcols - c0)
                    wsl = xsl[:, c0:c0 + cw]
                    p.op("vector",
                         (lambda ws, cw_: lambda eng: eng.tensor_scalar(
                             sqf[:, :cw_], ws, swt[:, 0:1], None,
                             Alu.mult))(wsl, cw),
                         reads=("xz", swn), writes=("sq",))
                    p.op("scalar",
                         (lambda cw_: lambda eng: eng.activation(
                             stf[:, :cw_], sqf[:, :cw_], Act.Abs))(cw),
                         reads=("sq",), writes=("st",))
                    p.op("vector",
                         (lambda cw_: lambda eng: eng.tensor_scalar(
                             stf[:, :cw_], stf[:, :cw_], 0.5, None,
                             Alu.is_gt))(cw),
                         reads=("st",), writes=("st",))
                    p.op("scalar",
                         (lambda ws, cw_: lambda eng: eng.activation(
                             mxf[:, :cw_], ws, Act.Sign))(wsl, cw),
                         reads=("xz",), writes=("mx",))
                    p.op("vector",
                         (lambda qd, cw_, c0_: lambda eng: eng.tensor_tensor(
                             qd[:, c0_:c0_ + cw_], stf[:, :cw_], mxf[:, :cw_],
                             Alu.mult))(qsl, cw, c0),
                         reads=("st", "mx"), writes=("q2",))
                p.dma("scalar", f"wqs{half}",
                      (lambda qs_, c: lambda eng: eng.dma_start(wq_hh[c], qs_))(qsl, ch),
                      reads=("q2",), writes=())

        wquant(w1_h, wq1_h, DC, DF, sw1, "sw1")
        wquant(w2_h, wq2_h, FC, D, sw2, "sw2")

        x3 = x_sb[:, :, :]
        x13 = x1_sb[:, :, :]
        sq3 = sq_sb[:, :, :]
        st3 = st_sb[:, :, :]
        mx3 = mx_sb[:, :, :]
        ot3 = ot_sb[:, :, :]
        xi3 = xz_sb[:, 0:CC, :]
        sl3 = xz_sb[:, CC:2 * CC, :]
        bb3 = bb_sb[:, :, :]
        hh3 = hh_sb[:, :, :]

        def norm_stats(src_t, nch, srcbuf):
            nfeat = float(nch * 128)
            p.op("scalar",
                 (lambda s, n: lambda eng: eng.activation(
                     sq_sb[:, 0:n, :], s[:, 0:n, :], Act.Square))(src_t, nch),
                 reads=(srcbuf,), writes=("sq",))

            def sums(s=src_t, n=nch):
                def f(eng):
                    for c in range(n):
                        eng.matmul(prow[0:1, 0:TT], on_sb[:, 0:1], s[:, c, :],
                                   start=(c == 0), stop=False)
                    last = None
                    for c in range(n):
                        last = eng.matmul(prow[0:1, TT:2 * TT], on_sb[:, 0:1],
                                          sq_sb[:, c, :],
                                          start=False, stop=(c == n - 1))
                    return last
                return f
            p.op("tensor", sums(), reads=(srcbuf, "sq", "ones", "prow"),
                 writes=("prow",))
            p.op("vector",
                 (lambda nf: lambda eng: eng.tensor_scalar(
                     r_sm[:, :], prow[0:1, 0:TT], 1.0 / nf, None, Alu.mult))(nfeat),
                 reads=("prow",), writes=("rsm",))
            p.op("vector",
                 lambda eng: eng.tensor_mul(r_tmp[:, :], r_sm[:, :], r_sm[:, :]),
                 reads=("rsm",), writes=("rtmp",))
            p.op("vector",
                 (lambda nf: lambda eng: eng.scalar_tensor_tensor(
                     r_tmp[:, :], prow[0:1, TT:2 * TT], 1.0 / nf, r_tmp[:, :],
                     Alu.mult, Alu.subtract))(nfeat),
                 reads=("prow", "rtmp"), writes=("rtmp",))
            p.op("scalar",
                 lambda eng: eng.activation(r_rs[:, :], r_tmp[:, :], Act.Sqrt,
                                            bias=eps_sb[0:1, 0:1], scale=1.0),
                 reads=("rtmp", "eps"), writes=("rrs",))
            p.op("vector", lambda eng: eng.reciprocal(r_rs[:, :], r_rs[:, :]),
                 reads=("rrs",), writes=("rrs",))
            p.op("gpsimd",
                 lambda eng: eng.partition_broadcast(t_mm[:, :], r_sm[0:1, :], 128),
                 reads=("rsm",), writes=("mm",))
            p.op("gpsimd",
                 lambda eng: eng.partition_broadcast(t_rr[:, :], r_rs[0:1, :], 128),
                 reads=("rrs",), writes=("rr",))

        def x_load(kk):
            for b_ in range(2):
                p.dma("sync", f"x{b_}",
                      (lambda b2_, kt: lambda eng: eng.dma_start(
                          x_sb[:, :, b2_ * 128:(b2_ + 1) * 128], x_h[b2_, kt]))(b_, kk),
                      writes=("x",))

        def ln1_front(kk):
            norm_stats(x_sb, DC, "x")
            p.op("vector",
                 lambda eng: eng.tensor_tensor(sq3, x3, bc(t_mm, DC), Alu.subtract),
                 reads=("x", "mm"), writes=("sq",))
            p.op("vector",
                 lambda eng: eng.tensor_tensor(bb_sb[:, 0:DC, :], sq3, bc(t_rr, DC),
                                               Alu.mult),
                 reads=("sq", "rr"), writes=("bb",))

        def win_stage(kk):
            for jg in range(4):
                for k8 in range(DC):
                    li = jg * DC + k8
                    buf = winb[li % 2]
                    bn = f"winb{li % 2}"
                    p.dma("sync", bn,
                          (lambda bf, kc, j0: lambda eng: eng.dma_start(
                              bf[:, :], win_h[kc][:, j0 * 1024:(j0 + 1) * 1024]))(
                              buf, k8, jg),
                          writes=(bn,))

                    def mmg(bf=buf, k8_=k8):
                        def f(eng):
                            last = None
                            for j in range(8):
                                bank = pm[j // 2]
                                c0 = (j % 2) * TT
                                last = eng.matmul(
                                    bank[:, c0:c0 + TT],
                                    bf[:, j * 128:(j + 1) * 128],
                                    bb_sb[:, k8_, :],
                                    start=(k8_ == 0 and j % 2 == 0),
                                    stop=(k8_ == DC - 1 and j % 2 == 1))
                            return last
                        return f
                    p.op("tensor", mmg(),
                         reads=(bn, "bb", "pm0", "pm1", "pm2", "pm3"),
                         writes=("pm0", "pm1", "pm2", "pm3"))
                for bkn in range(4):
                    mglob = jg * 8 + 2 * bkn
                    if mglob < CC:
                        p.op("scalar",
                             (lambda bk, mg: lambda eng: eng.copy(
                                 xz_sb[:, mg:mg + 2, :], pm[bk][:, :]))(bkn, mglob),
                             reads=(f"pm{bkn}",), writes=("xz",))
                    else:
                        p.op("scalar",
                             (lambda bk, mg: lambda eng: eng.activation(
                                 xz_sb[:, mg:mg + 2, :], pm[bk][:, :],
                                 Act.Silu))(bkn, mglob),
                             reads=(f"pm{bkn}",), writes=("xz",))

        def mixer_mid(kk):
            def dmmg():
                def f(eng):
                    last = None
                    for k16 in range(CC):
                        last = eng.matmul(prow[0:1, 0:TT], wd_sb[:, k16:k16 + 1],
                                          xz_sb[:, k16, :],
                                          start=(k16 == 0), stop=(k16 == CC - 1))
                    return last
                return f
            p.op("tensor", dmmg(), reads=("wd", "xz", "prow"), writes=("prow",))

            def bcmmg():
                def f(eng):
                    last = None
                    for k16 in range(CC):
                        for hf in range(2):
                            last = eng.matmul(
                                pbc[:, hf * TT:(hf + 1) * TT],
                                wbc_sb[:, k16, hf * 64:(hf + 1) * 64],
                                xz_sb[:, k16, :],
                                start=(k16 == 0 and hf == 0),
                                stop=(k16 == CC - 1 and hf == 1))
                    return last
                return f
            p.op("tensor", bcmmg(), reads=("wbc", "xz", "pbc"), writes=("pbc",))
            p.op("scalar", lambda eng: eng.copy(braw[:, :], pbc[0:64, 0:TT]),
                 reads=("pbc",), writes=("braw",))
            p.op("scalar", lambda eng: eng.copy(craw[:, :], pbc[0:64, TT:2 * TT]),
                 reads=("pbc",), writes=("craw",))

            p.op("scalar",
                 lambda eng: eng.activation(r_ab[:, :], prow[0:1, 0:TT], Act.Abs),
                 reads=("prow",), writes=("rab",))
            p.op("scalar",
                 lambda eng: eng.activation(r_en[:, :], r_ab[:, :], Act.Exp,
                                            bias=0.0, scale=-1.0),
                 reads=("rab",), writes=("ren",))
            p.op("scalar",
                 lambda eng: eng.activation(r_ln[:, :], r_en[:, :], Act.Ln,
                                            bias=1.0, scale=1.0),
                 reads=("ren",), writes=("rln",))
            p.op("vector",
                 lambda eng: eng.tensor_scalar(r_mx[:, :], prow[0:1, 0:TT], 0.0,
                                               None, Alu.max),
                 reads=("prow",), writes=("rmx",))
            p.op("vector",
                 lambda eng: eng.tensor_add(r_dl[:, :], r_mx[:, :], r_ln[:, :]),
                 reads=("rmx", "rln"), writes=("rdl",))
            p.op("scalar",
                 lambda eng: eng.activation(r_ar[:, :], r_dl[:, :], Act.Exp,
                                            bias=0.0, scale=an_sb[0:1, 0:1]),
                 reads=("rdl", "aneg"), writes=("rar",))
            p.op("gpsimd",
                 lambda eng: eng.partition_broadcast(abc_sb[:, :], r_ar[0:1, :], 128),
                 reads=("rar",), writes=("abc",))

            for pr in range(8):
                cc0 = 2 * pr
                pair = slice(cc0, cc0 + 2)

                def bmm(c0=cc0):
                    def f(eng):
                        eng.matmul(pB[:, 0:TT], wB_sb[:, c0, :], braw[:, :],
                                   start=True, stop=False)
                        return eng.matmul(pB[:, TT:2 * TT], wB_sb[:, c0 + 1, :],
                                          braw[:, :], start=False, stop=True)
                    return f

                def cmm(c0=cc0):
                    def f(eng):
                        eng.matmul(pC[:, 0:TT], wC_sb[:, c0, :], craw[:, :],
                                   start=True, stop=False)
                        return eng.matmul(pC[:, TT:2 * TT], wC_sb[:, c0 + 1, :],
                                          craw[:, :], start=False, stop=True)
                    return f
                p.op("tensor", bmm(), reads=("wB", "braw", "pB"), writes=("pB",))
                p.op("tensor", cmm(), reads=("wC", "craw", "pC"), writes=("pC",))
                p.op("vector",
                     (lambda c0: lambda eng: eng.tensor_tensor(
                         bb_sb[:, c0:c0 + 2, :].rearrange("p c t -> p (c t)"),
                         pB[:, :],
                         xz_sb[:, c0:c0 + 2, :].rearrange("p c t -> p (c t)"),
                         Alu.mult))(cc0),
                     reads=("pB", "xz"), writes=("bb",))
                for ci in (cc0, cc0 + 1):
                    for sq_ in range(2):
                        t0 = sq_ * 128
                        if kk == 0:
                            init = 0.0
                        else:
                            init = carry[:, 2 * ci + sq_: 2 * ci + sq_ + 1]
                        p.op("vector",
                             (lambda c_, tt0, ini: lambda eng: eng.tensor_tensor_scan(
                                 hh_sb[:, c_, tt0:tt0 + 128],
                                 abc_sb[:, tt0:tt0 + 128],
                                 bb_sb[:, c_, tt0:tt0 + 128],
                                 ini, Alu.mult, Alu.add))(ci, t0, init),
                             reads=("abc", "bb", "carry"), writes=("hh",))
                p.op("vector",
                     (lambda c0: lambda eng: eng.tensor_copy(
                         carry[:, 2 * c0:2 * c0 + 4],
                         hh_sb[:, :, :].rearrange("p c t -> p (c t)")
                         [:, c0 * TT + 127::128][:, 0:4]))(cc0),
                     reads=("hh",), writes=("carry",))
                p.op("vector",
                     (lambda c0: lambda eng: eng.tensor_tensor(
                         bb_sb[:, c0:c0 + 2, :].rearrange("p c t -> p (c t)"),
                         pC[:, :],
                         hh_sb[:, c0:c0 + 2, :].rearrange("p c t -> p (c t)"),
                         Alu.mult))(cc0),
                     reads=("pC", "hh"), writes=("bb",))
                dD_b = dD_sb[:, pair].unsqueeze(2).broadcast_to([128, 2, TT])
                p.op("vector",
                     (lambda pa, db, pr_: lambda eng: eng.tensor_tensor(
                         sq_sb[:, 2 + (pr_ % 2) * 2:4 + (pr_ % 2) * 2, :],
                         xz_sb[:, pa, :], db, Alu.mult))(pair, dD_b, pr),
                     reads=("xz", "dD"), writes=("sq",))
                p.op("vector",
                     (lambda pa, pr_: lambda eng: eng.tensor_tensor(
                         sq_sb[:, 2 + (pr_ % 2) * 2:4 + (pr_ % 2) * 2, :],
                         bb_sb[:, pa, :],
                         sq_sb[:, 2 + (pr_ % 2) * 2:4 + (pr_ % 2) * 2, :],
                         Alu.add))(pair, pr),
                     reads=("bb", "sq"), writes=("sq",))
                p.op("vector",
                     (lambda pa, c0, pr_: lambda eng: eng.tensor_tensor(
                         hh_sb[:, pa, :],
                         sq_sb[:, 2 + (pr_ % 2) * 2:4 + (pr_ % 2) * 2, :],
                         xz_sb[:, CC + c0:CC + c0 + 2, :], Alu.mult))(pair, cc0, pr),
                     reads=("sq", "xz"), writes=("hh",))
                for mg in range(2):
                    for k16 in (cc0, cc0 + 1):
                        li = (pr * 4 + mg * 2 + (k16 - cc0)) % 2
                        buf = wob[li]
                        bn = f"wob{li}"
                        p.dma("sync", bn,
                              (lambda bf, kc, m0: lambda eng: eng.dma_start(
                                  bf[:, :], wo_h[kc][:, m0 * 512:(m0 + 1) * 512]))(
                                  buf, k16, mg),
                              writes=(bn,))

                        def womm(bf=buf, k16_=k16, mg_=mg):
                            def f(eng):
                                last = None
                                for j in range(4):
                                    bank = pm[2 * mg_ + j // 2]
                                    c0_ = (j % 2) * TT
                                    last = eng.matmul(
                                        bank[:, c0_:c0_ + TT],
                                        bf[:, j * 128:(j + 1) * 128],
                                        hh_sb[:, k16_, :],
                                        start=(k16_ == 0 and j % 2 == 0),
                                        stop=(k16_ == CC - 1 and j % 2 == 1))
                                return last
                            return f
                        p.op("tensor", womm(),
                             reads=(bn, "hh", f"pm{2 * mg}", f"pm{2 * mg + 1}"),
                             writes=(f"pm{2 * mg}", f"pm{2 * mg + 1}"))
            for bkn in range(4):
                mglob = 2 * bkn
                p.op("scalar",
                     (lambda b_, mg: lambda eng: eng.copy(
                         mx_sb[:, mg:mg + 2, :], pm[b_][:, :]))(bkn, mglob),
                     reads=(f"pm{bkn}",), writes=("mx",))

        def ln2_u(kk):
            norm_stats(mx_sb, DC, "mx")
            p.op("vector",
                 lambda eng: eng.tensor_tensor(sq3, mx3, bc(t_mm, DC), Alu.subtract),
                 reads=("mx", "mm"), writes=("sq",))
            p.op("vector",
                 lambda eng: eng.tensor_tensor(st3, sq3, bc(t_rr, DC), Alu.mult),
                 reads=("sq", "rr"), writes=("st",))
            p.op("vector",
                 lambda eng: eng.tensor_tensor(x13, st3, x3, Alu.add),
                 reads=("st", "x"), writes=("x1",))

        def ln3_q1(kk):
            norm_stats(x1_sb, DC, "x1")
            p.op("vector",
                 lambda eng: eng.tensor_tensor(sq3, x13, bc(t_mm, DC), Alu.subtract),
                 reads=("x1", "mm"), writes=("sq",))
            p.op("vector",
                 lambda eng: eng.tensor_tensor(mx3, sq3, bc(t_rr, DC), Alu.mult),
                 reads=("sq", "rr"), writes=("mx",))
            p.op("scalar",
                 lambda eng: eng.activation(sq3, mx3, Act.Square),
                 reads=("mx",), writes=("sq",))

            def sq_sums():
                def f(eng):
                    last = None
                    for c in range(DC):
                        last = eng.matmul(prow[0:1, 0:TT], on_sb[:, 0:1],
                                          sq_sb[:, c, :],
                                          start=(c == 0), stop=(c == DC - 1))
                    return last
                return f
            p.op("tensor", sq_sums(), reads=("sq", "ones", "prow"),
                 writes=("prow",))
            p.op("vector",
                 lambda eng: eng.tensor_reduce(
                     t_ms[:, :], sq3.transpose([0, 2, 1]), AX.X, Alu.max),
                 reads=("sq",), writes=("tms",))
            p.op("gpsimd",
                 lambda eng: eng.partition_all_reduce(
                     t_ax[:, :], t_ms[:, :], 128, ReduceOp.max),
                 reads=("tms",), writes=("ax",))
            p.op("scalar",
                 lambda eng: eng.activation(r_rs[:, :], prow[0:1, 0:TT], Act.Sqrt,
                                            bias=eps_sb[0:1, 0:1], scale=1.0 / D),
                 reads=("prow", "eps"), writes=("rrs",))
            p.op("vector", lambda eng: eng.reciprocal(r_rs[:, :], r_rs[:, :]),
                 reads=("rrs",), writes=("rrs",))
            p.op("gpsimd",
                 lambda eng: eng.partition_broadcast(t_rr[:, :], r_rs[0:1, :], 128),
                 reads=("rrs",), writes=("rr",))
            p.op("scalar",
                 lambda eng: eng.activation(t_ax[:, :], t_ax[:, :], Act.Sqrt),
                 reads=("ax",), writes=("ax",))
            p.op("vector",
                 lambda eng: eng.tensor_mul(t_ax[:, :], t_ax[:, :], t_rr[:, :]),
                 reads=("ax", "rr"), writes=("ax",))
            p.op("vector",
                 lambda eng: eng.tensor_scalar(t_ax[:, :], t_ax[:, :], 1e-5, None,
                                               Alu.max),
                 reads=("ax",), writes=("ax",))
            p.op("vector", lambda eng: eng.reciprocal(t_rc[:, :], t_ax[:, :]),
                 reads=("ax",), writes=("rc",))
            p.op("vector",
                 lambda eng: eng.scalar_tensor_tensor(
                     t_sc[:, :], t_rr[:, :], 127.0, t_rc[:, :], Alu.mult, Alu.mult),
                 reads=("rr", "rc"), writes=("sc",))
            p.op("vector",
                 lambda eng: eng.tensor_scalar(t_fs[:, :], t_ax[:, :], m1c[:, 0:1],
                                               1.0 / 127.0, Alu.mult, Alu.mult),
                 reads=("ax", "m1c"), writes=("fs",))
            p.op("vector",
                 lambda eng: eng.tensor_tensor(hh_sb[:, 0:DC, :], mx3, bc(t_sc, DC),
                                               Alu.mult),
                 reads=("mx", "sc"), writes=("hh",))
            p.op("scalar",
                 lambda eng: eng.activation(sq3, hh_sb[:, 0:DC, :], Act.Copy,
                                            bias=C32, scale=1.0),
                 reads=("hh",), writes=("sq",))
            p.op("scalar",
                 lambda eng: eng.activation(st3, hh_sb[:, 0:DC, :], Act.Copy,
                                            bias=C64, scale=1.0),
                 reads=("hh",), writes=("st",))
            p.op("vector",
                 lambda eng: eng.scalar_tensor_tensor(
                     st3, st3, -C64, hh_sb[:, 0:DC, :], Alu.add, Alu.subtract),
                 reads=("st", "hh"), writes=("st",))
            p.op("vector",
                 lambda eng: eng.scalar_tensor_tensor(
                     st3, st3, fF[:, 0:1], sq3, Alu.mult, Alu.add),
                 reads=("st", "fF", "sq"), writes=("st",))
            p.op("vector",
                 lambda eng: eng.tensor_scalar(q1_sb[:, :, :], st3, -C32, None,
                                               Alu.add),
                 reads=("st",), writes=("q1",))

        def ffn1_stage(kk):
            for jg in range(4):
                for k8 in range(DC):
                    li = jg * DC + k8
                    buf = wq1b[li % 2]
                    bn = f"wq1b{li % 2}"
                    p.dma("sync", bn,
                          (lambda bf, kc, j0: lambda eng: eng.dma_start(
                              bf[:, :], wq1_h[kc][:, j0 * 1024:(j0 + 1) * 1024]))(
                              buf, k8, jg),
                          writes=(bn,))

                    def f1mm(bf=buf, k8_=k8):
                        def f(eng):
                            last = None
                            for j in range(8):
                                bank = pm[j // 2]
                                c0 = (j % 2) * TT
                                last = eng.matmul(
                                    bank[:, c0:c0 + TT],
                                    bf[:, j * 128:(j + 1) * 128],
                                    q1_sb[:, k8_, :],
                                    start=(k8_ == 0 and j % 2 == 0),
                                    stop=(k8_ == DC - 1 and j % 2 == 1))
                            return last
                        return f
                    p.op("tensor", f1mm(),
                         reads=(bn, "q1", "pm0", "pm1", "pm2", "pm3"),
                         writes=("pm0", "pm1", "pm2", "pm3"))
                for bkn in range(4):
                    mglob = jg * 8 + 2 * bkn
                    fs_b1 = t_fs[:, :].unsqueeze(1).broadcast_to([128, 1, TT])
                    for half in range(2):
                        mg1 = mglob + half
                        slot = sq_sb[:, 2 * (bkn % 2) + half, :]
                        p.op("vector",
                             (lambda bk, sl, hf: lambda eng:
                              eng.scalar_tensor_tensor(
                                  sl, pm[bk][:, hf * TT:(hf + 1) * TT], 0.0,
                                  t_fs[:, :], Alu.bypass, Alu.mult))(
                                  bkn, slot, half),
                             reads=(f"pm{bkn}", "fs"), writes=("sq",))
                        p.op("scalar",
                             (lambda sl, mg: lambda eng: eng.activation(
                                 xz_sb[:, mg, :], sl, Act.Gelu,
                                 bias=b1_sb[:, mg:mg + 1], scale=1.0))(slot, mg1),
                             reads=("sq", "b1c"), writes=("xz",))

        def stats2_q2(kk):
            for pc in range(4):
                gsl = xz_sb[:, pc * 8:(pc + 1) * 8, :]
                p.op("scalar",
                     (lambda gs: lambda eng: eng.activation(
                         sq3, gs, Act.Square))(gsl),
                     reads=("xz",), writes=("sq",))

                def sq_sums2(pc_=pc):
                    def f(eng):
                        last = None
                        for c in range(DC):
                            last = eng.matmul(
                                prow[0:1, 0:TT], on_sb[:, 0:1], sq_sb[:, c, :],
                                start=(pc_ == 0 and c == 0),
                                stop=(pc_ == 3 and c == DC - 1))
                        return last
                    return f
                p.op("tensor", sq_sums2(), reads=("sq", "ones", "prow"),
                     writes=("prow",))
                p.op("vector",
                     lambda eng: eng.tensor_reduce(
                         t_ms[:, :], sq3.transpose([0, 2, 1]), AX.X, Alu.max),
                     reads=("sq",), writes=("tms",))
                if pc == 0:
                    p.op("vector",
                         lambda eng: eng.tensor_copy(t_am[:, :], t_ms[:, :]),
                         reads=("tms",), writes=("am",))
                else:
                    p.op("vector",
                         lambda eng: eng.tensor_tensor(t_am[:, :], t_am[:, :],
                                                       t_ms[:, :], Alu.max),
                         reads=("tms", "am"), writes=("am",))
            p.op("gpsimd",
                 lambda eng: eng.partition_all_reduce(
                     t_ax[:, :], t_am[:, :], 128, ReduceOp.max),
                 reads=("am",), writes=("ax",))
            p.op("scalar",
                 lambda eng: eng.activation(r_rs[:, :], prow[0:1, 0:TT], Act.Sqrt,
                                            bias=eps_sb[0:1, 0:1], scale=1.0 / DF),
                 reads=("prow", "eps"), writes=("rrs",))
            p.op("vector", lambda eng: eng.reciprocal(r_rs[:, :], r_rs[:, :]),
                 reads=("rrs",), writes=("rrs",))
            p.op("gpsimd",
                 lambda eng: eng.partition_broadcast(t_rr[:, :], r_rs[0:1, :], 128),
                 reads=("rrs",), writes=("rr",))
            p.op("scalar",
                 lambda eng: eng.activation(t_ax[:, :], t_ax[:, :], Act.Sqrt),
                 reads=("ax",), writes=("ax",))
            p.op("vector",
                 lambda eng: eng.tensor_mul(t_ax[:, :], t_ax[:, :], t_rr[:, :]),
                 reads=("ax", "rr"), writes=("ax",))
            p.op("vector",
                 lambda eng: eng.tensor_scalar(t_ax[:, :], t_ax[:, :], 1e-5, None,
                                               Alu.max),
                 reads=("ax",), writes=("ax",))
            p.op("vector", lambda eng: eng.reciprocal(t_rc[:, :], t_ax[:, :]),
                 reads=("ax",), writes=("rc",))
            p.op("vector",
                 lambda eng: eng.scalar_tensor_tensor(
                     t_sc[:, :], t_rr[:, :], 127.0, t_rc[:, :], Alu.mult, Alu.mult),
                 reads=("rr", "rc"), writes=("sc",))
            p.op("vector",
                 lambda eng: eng.tensor_scalar(t_fs[:, :], t_ax[:, :], m2c[:, 0:1],
                                               1.0 / 127.0, Alu.mult, Alu.mult),
                 reads=("ax", "m2c"), writes=("fs",))
            for pc in range(4):
                gsl = xz_sb[:, pc * 8:(pc + 1) * 8, :]
                p.op("vector",
                     (lambda gs: lambda eng: eng.tensor_tensor(
                         mx3, gs, bc(t_sc, DC), Alu.mult))(gsl),
                     reads=("xz", "sc"), writes=("mx",))
                p.op("scalar",
                     lambda eng: eng.activation(sq3, mx3, Act.Copy,
                                                bias=C32, scale=1.0),
                     reads=("mx",), writes=("sq",))
                p.op("scalar",
                     lambda eng: eng.activation(st3, mx3, Act.Copy,
                                                bias=C64, scale=1.0),
                     reads=("mx",), writes=("st",))
                p.op("vector",
                     lambda eng: eng.scalar_tensor_tensor(
                         st3, st3, -C64, mx3, Alu.add, Alu.subtract),
                     reads=("st", "mx"), writes=("st",))
                p.op("vector",
                     lambda eng: eng.scalar_tensor_tensor(
                         st3, st3, fF[:, 0:1], sq3, Alu.mult, Alu.add),
                     reads=("st", "fF", "sq"), writes=("st",))
                p.op("vector",
                     (lambda p_: lambda eng: eng.tensor_scalar(
                         q2_sb[:, p_ * 8:(p_ + 1) * 8, :], st3, -C32, None,
                         Alu.add))(pc),
                     reads=("st",), writes=("q2",))

        def ffn2_stage(kk):
            for k32 in range(FC):
                buf = wq2b[k32 % 2]
                bn = f"wq2b{k32 % 2}"
                p.dma("sync", bn,
                      (lambda bf, kc: lambda eng: eng.dma_start(
                          bf[:, :], wq2_h[kc]))(buf, k32),
                      writes=(bn,))

                def f2mm(bf=buf, k32_=k32):
                    def f(eng):
                        last = None
                        for j in range(8):
                            bank = pm[j // 2]
                            c0 = (j % 2) * TT
                            last = eng.matmul(
                                bank[:, c0:c0 + TT],
                                bf[:, j * 128:(j + 1) * 128],
                                q2_sb[:, k32_, :],
                                start=(k32_ == 0 and j % 2 == 0),
                                stop=(k32_ == FC - 1 and j % 2 == 1))
                        return last
                    return f
                p.op("tensor", f2mm(),
                     reads=(bn, "q2", "pm0", "pm1", "pm2", "pm3"),
                     writes=("pm0", "pm1", "pm2", "pm3"))

        def ffn2_evac(kk):
            for bkn in range(4):
                mglob = 2 * bkn
                fs_b = t_fs[:, :].unsqueeze(1).broadcast_to([128, 2, TT])
                p.op("vector",
                     (lambda bk, mg: lambda eng: eng.scalar_tensor_tensor(
                         mx_sb[:, mg:mg + 2, :],
                         pm[bk][:, :].rearrange("p (c t) -> p c t", c=2), 0.0,
                         fs_b,
                         Alu.bypass, Alu.mult))(bkn, mglob),
                     reads=(f"pm{bkn}", "fs"), writes=("mx",))
            b2_b = b2_sb[:, :].unsqueeze(2).broadcast_to([128, DC, TT])
            p.op("vector",
                 lambda eng: eng.tensor_tensor(mx3, mx3, b2_b, Alu.add),
                 reads=("mx", "b2c"), writes=("mx",))

        def ln4_out(kk):
            norm_stats(mx_sb, DC, "mx")
            p.op("vector",
                 lambda eng: eng.tensor_tensor(sq3, mx3, bc(t_mm, DC), Alu.subtract),
                 reads=("mx", "mm"), writes=("sq",))
            p.op("vector",
                 lambda eng: eng.tensor_tensor(st3, sq3, bc(t_rr, DC), Alu.mult),
                 reads=("sq", "rr"), writes=("st",))
            p.op("vector",
                 lambda eng: eng.tensor_tensor(ot3, st3, x13, Alu.add),
                 reads=("st", "x1"), writes=("ot",))
            for b_ in range(2):
                p.dma("scalar", f"out{b_}",
                      (lambda b2_, kt: lambda eng: eng.dma_start(
                          o_h[b2_, kt], ot_sb[:, :, b2_ * 128:(b2_ + 1) * 128]))(b_, kk),
                      reads=("ot",), writes=())

        x_load(0)
        ln1_front(0)
        win_stage(0)
        for k in range(n_tiles):
            mixer_mid(k)
            ln2_u(k)
            if k + 1 < n_tiles:
                x_load(k + 1)
            ln3_q1(k)
            ffn1_stage(k)
            stats2_q2(k)
            if k + 1 < n_tiles:
                ln1_front(k + 1)
            ffn2_stage(k)
            ffn2_evac(k)
            if k + 1 < n_tiles:
                win_stage(k + 1)
            ln4_out(k)

        p.final_wait("sync", "out0")
        p.final_wait("sync", "out1")

        _emit(nc, p, esems, dsems)

    m = {}
    for lib in all_libraries:
        for t in lib.instructions:
            m[t] = m.get(t, 0) | (1 << lib.index)
    _bass_rust.insert_library_loads(nc, m, len(all_libraries), standard.index)
    _bass_rust.insert_act_table_loads(nc)
    mybir.codegen_inst_isa_subclasses(nc)
    return nc


def make_in_map(core_id, inputs):
    x = np.asarray(inputs["L0_attn_x"])
    W_in = np.ascontiguousarray(np.asarray(inputs["L0_attn_W_in"], np.float32))
    W_xp = np.asarray(inputs["L0_attn_W_xp"])
    W_B = np.asarray(inputs["L0_attn_W_B"])
    W_C = np.asarray(inputs["L0_attn_W_C"])
    W_out = np.asarray(inputs["L0_attn_W_out"])
    Dv = np.asarray(inputs["L0_attn_D"])
    A_log = np.asarray(inputs["L0_attn_A_log"])
    w1 = np.asarray(inputs["L0_ffn_w1"])
    b1 = np.asarray(inputs["L0_ffn_b1"])
    w2 = np.asarray(inputs["L0_ffn_w2"])
    b2 = np.asarray(inputs["L0_ffn_b2"])

    xc = x[2 * core_id:2 * core_id + 2]
    xt = xc.transpose(0, 2, 1).reshape(2, DC, 128, NT, 128)
    xt = np.ascontiguousarray(xt.transpose(0, 3, 2, 1, 4)).astype(np.float32)

    return {
        "x": xt,
        "w_in": np.ascontiguousarray(W_in.reshape(DC, 128, 2 * DI), np.float32),
        "w_xd": np.ascontiguousarray(
            W_xp[:, 0].reshape(CC, 128).T, np.float32),
        "w_xbc": np.ascontiguousarray(
            W_xp[:, 1:129].reshape(CC, 128, 128).transpose(1, 0, 2), np.float32),
        "w_B": np.ascontiguousarray(
            W_B.reshape(64, CC, 128), np.float32),
        "w_C": np.ascontiguousarray(
            W_C.reshape(64, CC, 128), np.float32),
        "w_out": np.ascontiguousarray(W_out.reshape(CC, 128, D), np.float32),
        "d_vec": np.ascontiguousarray(Dv.reshape(CC, 128).T, np.float32),
        "a_log": np.asarray(A_log, np.float32).reshape(1, 1),
        "w1t": np.ascontiguousarray(w1.T.reshape(DC, 128, DF), np.float32),
        "b1c": np.ascontiguousarray(b1.reshape(FC, 128).T, np.float32),
        "w2t": np.ascontiguousarray(w2.T.reshape(FC, 128, D), np.float32),
        "b2c": np.ascontiguousarray(b2.reshape(DC, 128).T, np.float32),
    }


def assemble_out(res_list):
    outs = []
    for r in res_list:
        o = np.asarray(r["out"])
        o = o.transpose(0, 3, 2, 1, 4).reshape(2, D, 2048).transpose(0, 2, 1)
        outs.append(o)
    return np.ascontiguousarray(np.concatenate(outs, 0), dtype=np.float32)


def kernel(**inputs):
    nc = build_program()
    in_maps = [make_in_map(c, inputs) for c in range(8)]
    res = run_bass_kernel_spmd(nc, in_maps, core_ids=list(range(8)))
    return assemble_out(res.results)


# version 10
# speedup vs baseline: 1.0726x; 1.0726x over previous
import numpy as np

import bass_rust as _bass_rust
import concourse.bass as bass
import concourse.mybir as mybir
from concourse.bass_isa import ReduceOp
from concourse.bass_utils import run_bass_kernel_spmd
from concourse.library_config import all_libraries, standard

F32 = mybir.dt.float32
F16 = mybir.dt.float16
Alu = mybir.AluOpType
Act = mybir.ActivationFunctionType
AX = mybir.AxisListType

C32 = 12582912.0
C64 = 6755399441055744.0

NT = 16
TT = 256
D, DI, DF = 1024, 2048, 4096
DC, CC, FC = 8, 16, 32

ENGINES = ("tensor", "vector", "scalar", "gpsimd", "sync")


class B:

    def __init__(self):
        self.ops = {e: [] for e in ENGINES}
        self.ecount = {e: 0 for e in ENGINES}
        self.dcount = {}
        self.ver = {}
        self.readers = {}
        self.last_wait = {e: {} for e in ENGINES}
        self.dirty_w = {e: set() for e in ENGINES}
        self.dirty_r = {e: set() for e in ENGINES}

    def _collect(self, engine, reads, writes):
        waits = {}
        drain = False
        no_drain = engine == "tensor"

        def cum(sk, val):
            if sk[0] == "dma":
                return self.dcount.get(sk[1], 0)
            return val

        for b in reads:
            v = self.ver.get(b)
            if v is not None:
                sk, val = v
                if sk == ("eng", engine):
                    if b in self.dirty_w[engine] and not no_drain:
                        drain = True
                else:
                    val = cum(sk, val)
                    if waits.get(sk, -1) < val:
                        waits[sk] = val
        for b in writes:
            v = self.ver.get(b)
            if v is not None:
                sk, val = v
                if sk != ("eng", engine):
                    val = cum(sk, val)
                    if waits.get(sk, -1) < val:
                        waits[sk] = val
            for sk, val in self.readers.get(b, {}).items():
                if sk == ("eng", engine):
                    if engine == "gpsimd" and b in self.dirty_r[engine]:
                        drain = True
                else:
                    val = cum(sk, val)
                    if waits.get(sk, -1) < val:
                        waits[sk] = val
        wlist = []
        for sk, val in waits.items():
            if self.last_wait[engine].get(sk, -1) < val:
                wlist.append((sk, val))
                self.last_wait[engine][sk] = val
        return wlist, drain

    def op(self, engine, fn, reads=(), writes=()):
        wlist, drain = self._collect(engine, reads, writes)
        if drain:
            self.dirty_w[engine] = set()
            self.dirty_r[engine] = set()
        self.ecount[engine] += 1
        myval = self.ecount[engine]
        me = ("eng", engine)
        for b in writes:
            self.ver[b] = (me, myval)
            self.readers[b] = {}
            self.dirty_w[engine].add(b)
        for b in reads:
            self.readers.setdefault(b, {})[me] = myval
            self.dirty_r[engine].add(b)
        self.ops[engine].append(("op", drain, wlist, fn))

    def dma(self, engine, stream, fn, reads=(), writes=()):
        wlist, drain = self._collect(engine, reads, writes)
        if drain:
            self.dirty_w[engine] = set()
            self.dirty_r[engine] = set()
        cnt = self.dcount.get(stream, 0) + 16
        self.dcount[stream] = cnt
        sk = ("dma", stream)
        for b in writes:
            self.ver[b] = (sk, cnt)
            self.readers[b] = {}
        for b in reads:
            rd = self.readers.setdefault(b, {})
            if rd.get(sk, -1) < cnt:
                rd[sk] = cnt
        self.ops[engine].append(("dma", drain, wlist, fn, stream))

    def final_wait(self, engine, stream):
        self.ops[engine].append(
            ("wait", False, [(("dma", stream), self.dcount.get(stream, 0))], None)
        )


def _emit(nc, plan, esems, dsems):
    with nc.Block() as block:
        def section(ename):
            def run(eng):
                for item in plan.ops[ename]:
                    kind = item[0]
                    drain, wlist = item[1], item[2]
                    if drain:
                        eng.drain()
                    for sk, val in wlist:
                        sem = esems[sk[1]] if sk[0] == "eng" else dsems[sk[1]]
                        eng.wait_ge(sem, val)
                    if kind == "op":
                        ins = item[3](eng)
                        ins.then_inc(esems[ename], 1)
                    elif kind == "dma":
                        ins = item[3](eng)
                        ins.then_inc(dsems[item[4]], 16)
            return run

        block.tensor(section("tensor"))
        block.vector(section("vector"))
        block.scalar(section("scalar"))
        block.gpsimd(section("gpsimd"))
        block.sync(section("sync"))


def build_program(n_tiles=NT, debug=False):
    nc = bass.Bass("TRN2", target_bir_lowering=False,
                   detect_race_conditions=False)

    x_h = nc.dram_tensor("x", [2, NT, 128, DC, 128], F32, kind="ExternalInput")
    o_h = nc.dram_tensor("out", [2, NT, 128, DC, 128], F32, kind="ExternalOutput")
    win_h = nc.dram_tensor("w_in", [DC, 128, 2 * DI], F32, kind="ExternalInput")
    wd_h = nc.dram_tensor("w_xd", [128, CC], F32, kind="ExternalInput")
    wbc_h = nc.dram_tensor("w_xbc", [128, CC, 128], F32, kind="ExternalInput")
    wB_h = nc.dram_tensor("w_B", [64, CC, 128], F32, kind="ExternalInput")
    wC_h = nc.dram_tensor("w_C", [64, CC, 128], F32, kind="ExternalInput")
    wo_h = nc.dram_tensor("w_out", [CC, 128, D], F32, kind="ExternalInput")
    dD_h = nc.dram_tensor("d_vec", [128, CC], F32, kind="ExternalInput")
    al_h = nc.dram_tensor("a_log", [1, 1], F32, kind="ExternalInput")
    w1_h = nc.dram_tensor("w1t", [DC, 128, DF], F32, kind="ExternalInput")
    b1_h = nc.dram_tensor("b1c", [128, FC], F32, kind="ExternalInput")
    w2_h = nc.dram_tensor("w2t", [FC, 128, D], F32, kind="ExternalInput")
    b2_h = nc.dram_tensor("b2c", [128, DC], F32, kind="ExternalInput")
    wq1_h = nc.dram_tensor("wq1d", [DC, 128, DF], F16, kind="Internal")
    wq2_h = nc.dram_tensor("wq2d", [FC, 128, D], F16, kind="Internal")

    dbg = {}
    if debug:
        for nm, sh in [("d_x1", [128, DC, TT]), ("d_xi", [128, CC, TT]),
                       ("d_s", [128, CC, TT]), ("d_h", [128, CC, TT]),
                       ("d_y2", [128, CC, TT]), ("d_u", [128, DC, TT]),
                       ("d_q1", [128, DC, TT]), ("d_g", [128, FC, TT]),
                       ("d_abar", [1, TT]), ("d_mix", [128, DC, TT])]:
            dbg[nm] = nc.dram_tensor(nm, sh, F32, kind="ExternalOutput")

    from contextlib import ExitStack
    stack = ExitStack()
    e = stack.enter_context
    with stack:
        x_sb = e(nc.sbuf_tensor("x_sb", [128, DC, TT], F32))
        x1_sb = e(nc.sbuf_tensor("x1_sb", [128, DC, TT], F32))
        sq_sb = e(nc.sbuf_tensor("sq_sb", [128, DC, TT], F32))
        st_sb = e(nc.sbuf_tensor("st_sb", [128, DC, TT], F32))
        mx_sb = e(nc.sbuf_tensor("mx_sb", [128, DC, TT], F32))
        xz_sb = e(nc.sbuf_tensor("xz_sb", [128, 2 * CC, TT], F32))
        bb_sb = e(nc.sbuf_tensor("bb_sb", [128, CC, TT], F32))
        ot_sb = e(nc.sbuf_tensor("ot_sb", [128, DC, TT], F32))
        hh_sb = e(nc.sbuf_tensor("hh_sb", [128, CC, TT], F32))
        q1_sb = e(nc.sbuf_tensor("q1_sb", [128, DC, TT], F16))
        q2_sb = e(nc.sbuf_tensor("q2_sb", [128, FC, TT], F16))
        carry = e(nc.sbuf_tensor("carry", [128, 32], F32))
        braw = e(nc.sbuf_tensor("braw", [64, TT], F32))
        craw = e(nc.sbuf_tensor("craw", [64, TT], F32))
        abc_sb = e(nc.sbuf_tensor("abc_sb", [128, TT], F32))
        t_mm = e(nc.sbuf_tensor("t_mm", [128, TT], F32))
        t_rr = e(nc.sbuf_tensor("t_rr", [128, TT], F32))
        t_ms = e(nc.sbuf_tensor("t_ms", [128, TT], F32))
        t_ax = e(nc.sbuf_tensor("t_ax", [128, TT], F32))
        t_sc = e(nc.sbuf_tensor("t_sc", [128, TT], F32))
        t_fs = e(nc.sbuf_tensor("t_fs", [128, TT], F32))
        t_aq = e(nc.sbuf_tensor("t_aq", [128, TT], F32))
        t_am = e(nc.sbuf_tensor("t_am", [128, TT], F32))
        t_rc = e(nc.sbuf_tensor("t_rc", [128, TT], F32))
        r_ab = e(nc.sbuf_tensor("r_ab", [1, TT], F32))
        r_en = e(nc.sbuf_tensor("r_en", [1, TT], F32))
        r_ln = e(nc.sbuf_tensor("r_ln", [1, TT], F32))
        r_mx = e(nc.sbuf_tensor("r_mx", [1, TT], F32))
        r_dl = e(nc.sbuf_tensor("r_dl", [1, TT], F32))
        r_ar = e(nc.sbuf_tensor("r_ar", [1, TT], F32))
        wd_sb = e(nc.sbuf_tensor("wd_sb", [128, CC], F32))
        wbc_sb = e(nc.sbuf_tensor("wbc_sb", [128, CC, 128], F32))
        wB_sb = e(nc.sbuf_tensor("wB_sb", [64, CC, 128], F32))
        wC_sb = e(nc.sbuf_tensor("wC_sb", [64, CC, 128], F32))
        dD_sb = e(nc.sbuf_tensor("dD_sb", [128, CC], F32))
        b1_sb = e(nc.sbuf_tensor("b1_sb", [128, FC], F32))
        b2_sb = e(nc.sbuf_tensor("b2_sb", [128, DC], F32))
        al_sb = e(nc.sbuf_tensor("al_sb", [1, 1], F32))
        an_sb = e(nc.sbuf_tensor("an_sb", [1, 1], F32))
        fF = e(nc.sbuf_tensor("fF", [128, 1], F32))
        eps_sb = e(nc.sbuf_tensor("eps_sb", [128, 1], F32))
        on_sb = e(nc.sbuf_tensor("on_sb", [128, 1], F32))
        r_sm = e(nc.sbuf_tensor("r_sm", [1, TT], F32))
        r_tmp = e(nc.sbuf_tensor("r_tmp", [1, TT], F32))
        r_rs = e(nc.sbuf_tensor("r_rs", [1, TT], F32))
        m1c = e(nc.sbuf_tensor("m1c", [128, 1], F32))
        m2c = e(nc.sbuf_tensor("m2c", [128, 1], F32))
        sw1 = e(nc.sbuf_tensor("sw1", [128, 1], F32))
        sw2 = e(nc.sbuf_tensor("sw2", [128, 1], F32))
        macc = e(nc.sbuf_tensor("macc", [128, 48], F32))
        winb = [e(nc.sbuf_tensor(f"winb{i}", [128, 1024], F32)) for i in (0, 1)]
        wob = [e(nc.sbuf_tensor(f"wob{i}", [128, 512], F32)) for i in (0, 1)]
        wq1b = [e(nc.sbuf_tensor(f"wq1b{i}", [128, 1024], F16)) for i in (0, 1)]
        wq2b = [e(nc.sbuf_tensor(f"wq2b{i}", [128, 1024], F16)) for i in (0, 1)]
        pm = [e(nc.psum_tensor(f"pm{i}", [128, 512], F32)) for i in range(4)]
        prow = e(nc.psum_tensor("prow", [1, 2 * TT], F32))
        pbc = e(nc.psum_tensor("pbc", [64, 512], F32))
        pB = e(nc.psum_tensor("pB", [128, 512], F32))
        pC = e(nc.psum_tensor("pC", [128, 512], F32))
        esems = {en: e(nc.semaphore(f"se_{en}")) for en in ENGINES}
        dstreams = (["x0", "x1", "out0", "out1", "xzld", "wqs0", "wqs1"]
                    + ["winb0", "winb1", "wob0", "wob1", "wq1b0", "wq1b1",
                       "wq2b0", "wq2b1"]
                    + ["r_wd", "r_wbc", "r_wB", "r_wC", "r_dD", "r_b1c",
                       "r_b2c", "r_alog"])
        dsems = {s: e(nc.semaphore(f"sd_{s}")) for s in dstreams}

        p = B()

        def bc(tile_ap, nch):
            return tile_ap[:, :].unsqueeze(1).broadcast_to([128, nch, TT])

        for (dst, src, bufn, strm) in [
            (wd_sb, wd_h, "wd", "r_wd"), (wbc_sb, wbc_h, "wbc", "r_wbc"),
            (wB_sb, wB_h, "wB", "r_wB"), (wC_sb, wC_h, "wC", "r_wC"),
            (dD_sb, dD_h, "dD", "r_dD"), (b1_sb, b1_h, "b1c", "r_b1c"),
            (b2_sb, b2_h, "b2c", "r_b2c"), (al_sb, al_h, "alog", "r_alog"),
        ]:
            p.dma("sync", strm,
                  (lambda d, s: lambda eng: eng.dma_start(d[...], s[...]))(dst, src),
                  writes=(bufn,))
        p.op("vector", lambda eng: eng.memset(fF[:, :], 1.0), writes=("fF",))
        p.op("vector",
             lambda eng: eng.tensor_scalar(fF[:, :], fF[:, :], C64, -C64, Alu.add, Alu.add),
             reads=("fF",), writes=("fF",))
        p.op("scalar",
             lambda eng: eng.activation(an_sb[:, :], al_sb[:, :], Act.Exp),
             reads=("alog",), writes=("aneg",))
        p.op("vector",
             lambda eng: eng.tensor_scalar(an_sb[:, :], an_sb[:, :], -1.0, None, Alu.mult),
             reads=("aneg",), writes=("aneg",))
        p.op("vector", lambda eng: eng.memset(carry[:, :], 0.0), writes=("carry",))
        p.op("vector", lambda eng: eng.memset(eps_sb[:, :], 1e-5), writes=("eps",))
        p.op("vector", lambda eng: eng.memset(on_sb[:, :], 1.0), writes=("ones",))

        xzf = xz_sb[:, :, :].rearrange("p c t -> p (c t)")
        q2f = q2_sb[:, :, :].rearrange("p c t -> p (c t)")

        def wmean(w_h, nch, wcols, mcol0, mname):
            for ch in range(nch):
                half = ch % 2
                p.dma("sync", "xzld",
                      (lambda hf, c: lambda eng: eng.dma_start(
                          xzf[:, hf * 4096:hf * 4096 + wcols], w_h[c]))(half, ch),
                      writes=("xz",))
                p.op("vector",
                     (lambda hf, c: lambda eng: eng.tensor_reduce(
                         macc[:, mcol0 + c:mcol0 + c + 1],
                         xzf[:, hf * 4096:hf * 4096 + wcols],
                         AX.X, Alu.max if False else Alu.add,
                         apply_absolute_value=True))(half, ch),
                     reads=("xz",), writes=(mname,))
            p.op("vector",
                 lambda eng: eng.tensor_reduce(
                     macc[:, mcol0:mcol0 + 1], macc[:, mcol0:mcol0 + nch],
                     AX.X, Alu.add),
                 reads=(mname,), writes=(mname,))
            p.op("gpsimd",
                 lambda eng: eng.partition_all_reduce(
                     macc[:, mcol0:mcol0 + 1], macc[:, mcol0:mcol0 + 1],
                     128, ReduceOp.add),
                 reads=(mname,), writes=(mname,))

        wmean(w1_h, DC, DF, 0, "macc1")
        wmean(w2_h, FC, D, 8, "macc2")
        p.op("vector",
             lambda eng: eng.tensor_scalar(m1c[:, :], macc[:, 0:1],
                                           1.0 / (DF * D), 1e-5, Alu.mult, Alu.max),
             reads=("macc1",), writes=("m1c",))
        p.op("vector", lambda eng: eng.reciprocal(sw1[:, :], m1c[:, :]),
             reads=("m1c",), writes=("sw1",))
        p.op("vector",
             lambda eng: eng.tensor_scalar(m2c[:, :], macc[:, 8:9],
                                           1.0 / (DF * D), 1e-5, Alu.mult, Alu.max),
             reads=("macc2",), writes=("m2c",))
        p.op("vector", lambda eng: eng.reciprocal(sw2[:, :], m2c[:, :]),
             reads=("m2c",), writes=("sw2",))

        sqf = sq_sb[:, :, :].rearrange("p c t -> p (c t)")
        stf = st_sb[:, :, :].rearrange("p c t -> p (c t)")
        mxf = mx_sb[:, :, :].rearrange("p c t -> p (c t)")

        def wquant(w_h, wq_hh, nch, wcols, swt, swn):
            npieces = (wcols + 2047) // 2048
            for ch in range(nch):
                half = ch % 2
                xsl = xzf[:, half * 4096: half * 4096 + wcols]
                qsl = q2f[:, half * 4096: half * 4096 + wcols]
                p.dma("sync", "xzld",
                      (lambda xs, c: lambda eng: eng.dma_start(xs, w_h[c]))(xsl, ch),
                      writes=("xz",))
                for pc in range(npieces):
                    c0 = pc * 2048
                    cw = min(2048, wcols - c0)
                    wsl = xsl[:, c0:c0 + cw]
                    p.op("vector",
                         (lambda ws, cw_: lambda eng: eng.tensor_scalar(
                             sqf[:, :cw_], ws, swt[:, 0:1], None,
                             Alu.mult))(wsl, cw),
                         reads=("xz", swn), writes=("sq",))
                    p.op("scalar",
                         (lambda cw_: lambda eng: eng.activation(
                             stf[:, :cw_], sqf[:, :cw_], Act.Abs))(cw),
                         reads=("sq",), writes=("st",))
                    p.op("vector",
                         (lambda cw_: lambda eng: eng.tensor_scalar(
                             stf[:, :cw_], stf[:, :cw_], 0.5, None,
                             Alu.is_gt))(cw),
                         reads=("st",), writes=("st",))
                    p.op("scalar",
                         (lambda ws, cw_: lambda eng: eng.activation(
                             mxf[:, :cw_], ws, Act.Sign))(wsl, cw),
                         reads=("xz",), writes=("mx",))
                    p.op("vector",
                         (lambda qd, cw_, c0_: lambda eng: eng.tensor_tensor(
                             qd[:, c0_:c0_ + cw_], stf[:, :cw_], mxf[:, :cw_],
                             Alu.mult))(qsl, cw, c0),
                         reads=("st", "mx"), writes=("q2",))
                p.dma("scalar", f"wqs{half}",
                      (lambda qs_, c: lambda eng: eng.dma_start(wq_hh[c], qs_))(qsl, ch),
                      reads=("q2",), writes=())

        wquant(w1_h, wq1_h, DC, DF, sw1, "sw1")
        wquant(w2_h, wq2_h, FC, D, sw2, "sw2")

        x3 = x_sb[:, :, :]
        x13 = x1_sb[:, :, :]
        sq3 = sq_sb[:, :, :]
        st3 = st_sb[:, :, :]
        mx3 = mx_sb[:, :, :]
        ot3 = ot_sb[:, :, :]
        xi3 = xz_sb[:, 0:CC, :]
        sl3 = xz_sb[:, CC:2 * CC, :]
        bb3 = bb_sb[:, :, :]
        hh3 = hh_sb[:, :, :]

        def norm_stats(src_t, nch, srcbuf):
            nfeat = float(nch * 128)
            p.op("scalar",
                 (lambda s, n: lambda eng: eng.activation(
                     sq_sb[:, 0:n, :], s[:, 0:n, :], Act.Square))(src_t, nch),
                 reads=(srcbuf,), writes=("sq",))

            def sums(s=src_t, n=nch):
                def f(eng):
                    for c in range(n):
                        eng.matmul(prow[0:1, 0:TT], on_sb[:, 0:1], s[:, c, :],
                                   start=(c == 0), stop=False)
                    last = None
                    for c in range(n):
                        last = eng.matmul(prow[0:1, TT:2 * TT], on_sb[:, 0:1],
                                          sq_sb[:, c, :],
                                          start=False, stop=(c == n - 1))
                    return last
                return f
            p.op("tensor", sums(), reads=(srcbuf, "sq", "ones", "prow"),
                 writes=("prow",))
            p.op("vector",
                 (lambda nf: lambda eng: eng.tensor_scalar(
                     r_sm[:, :], prow[0:1, 0:TT], 1.0 / nf, None, Alu.mult))(nfeat),
                 reads=("prow",), writes=("rsm",))
            p.op("vector",
                 lambda eng: eng.tensor_mul(r_tmp[:, :], r_sm[:, :], r_sm[:, :]),
                 reads=("rsm",), writes=("rtmp",))
            p.op("vector",
                 (lambda nf: lambda eng: eng.scalar_tensor_tensor(
                     r_tmp[:, :], prow[0:1, TT:2 * TT], 1.0 / nf, r_tmp[:, :],
                     Alu.mult, Alu.subtract))(nfeat),
                 reads=("prow", "rtmp"), writes=("rtmp",))
            p.op("scalar",
                 lambda eng: eng.activation(r_rs[:, :], r_tmp[:, :], Act.Ln,
                                            bias=eps_sb[0:1, 0:1], scale=1.0),
                 reads=("rtmp", "eps"), writes=("rrs",))
            p.op("scalar",
                 lambda eng: eng.activation(r_rs[:, :], r_rs[:, :], Act.Exp,
                                            bias=0.0, scale=-0.5),
                 reads=("rrs",), writes=("rrs",))
            p.op("gpsimd",
                 lambda eng: eng.partition_broadcast(t_mm[:, :], r_sm[0:1, :], 128),
                 reads=("rsm",), writes=("mm",))
            p.op("gpsimd",
                 lambda eng: eng.partition_broadcast(t_rr[:, :], r_rs[0:1, :], 128),
                 reads=("rrs",), writes=("rr",))

        def x_load(kk):
            for b_ in range(2):
                p.dma("sync", f"x{b_}",
                      (lambda b2_, kt: lambda eng: eng.dma_start(
                          x_sb[:, :, b2_ * 128:(b2_ + 1) * 128], x_h[b2_, kt]))(b_, kk),
                      writes=("x",))

        def ln1_front(kk):
            norm_stats(x_sb, DC, "x")
            p.op("vector",
                 lambda eng: eng.tensor_tensor(sq3, x3, bc(t_mm, DC), Alu.subtract),
                 reads=("x", "mm"), writes=("sq",))
            p.op("vector",
                 lambda eng: eng.tensor_tensor(bb_sb[:, 0:DC, :], sq3, bc(t_rr, DC),
                                               Alu.mult),
                 reads=("sq", "rr"), writes=("bb",))

        def win_stage(kk):
            for jg in range(4):
                for k8 in range(DC):
                    li = jg * DC + k8
                    buf = winb[li % 2]
                    bn = f"winb{li % 2}"
                    p.dma("sync", bn,
                          (lambda bf, kc, j0: lambda eng: eng.dma_start(
                              bf[:, :], win_h[kc][:, j0 * 1024:(j0 + 1) * 1024]))(
                              buf, k8, jg),
                          writes=(bn,))

                    def mmg(bf=buf, k8_=k8):
                        def f(eng):
                            last = None
                            for j in range(8):
                                bank = pm[j // 2]
                                c0 = (j % 2) * TT
                                last = eng.matmul(
                                    bank[:, c0:c0 + TT],
                                    bf[:, j * 128:(j + 1) * 128],
                                    bb_sb[:, k8_, :],
                                    start=(k8_ == 0 and j % 2 == 0),
                                    stop=(k8_ == DC - 1 and j % 2 == 1))
                            return last
                        return f
                    p.op("tensor", mmg(),
                         reads=(bn, "bb", "pm0", "pm1", "pm2", "pm3"),
                         writes=("pm0", "pm1", "pm2", "pm3"))
                for bkn in range(4):
                    mglob = jg * 8 + 2 * bkn
                    if mglob < CC:
                        p.op("scalar",
                             (lambda bk, mg: lambda eng: eng.copy(
                                 xz_sb[:, mg:mg + 2, :], pm[bk][:, :]))(bkn, mglob),
                             reads=(f"pm{bkn}",), writes=("xz",))
                    else:
                        p.op("scalar",
                             (lambda bk, mg: lambda eng: eng.activation(
                                 xz_sb[:, mg:mg + 2, :], pm[bk][:, :],
                                 Act.Silu))(bkn, mglob),
                             reads=(f"pm{bkn}",), writes=("xz",))

        def mixer_mid(kk):
            def dmmg():
                def f(eng):
                    last = None
                    for k16 in range(CC):
                        last = eng.matmul(prow[0:1, 0:TT], wd_sb[:, k16:k16 + 1],
                                          xz_sb[:, k16, :],
                                          start=(k16 == 0), stop=(k16 == CC - 1))
                    return last
                return f
            p.op("tensor", dmmg(), reads=("wd", "xz", "prow"), writes=("prow",))

            def bcmmg():
                def f(eng):
                    last = None
                    for k16 in range(CC):
                        for hf in range(2):
                            last = eng.matmul(
                                pbc[:, hf * TT:(hf + 1) * TT],
                                wbc_sb[:, k16, hf * 64:(hf + 1) * 64],
                                xz_sb[:, k16, :],
                                start=(k16 == 0 and hf == 0),
                                stop=(k16 == CC - 1 and hf == 1))
                    return last
                return f
            p.op("tensor", bcmmg(), reads=("wbc", "xz", "pbc"), writes=("pbc",))
            p.op("scalar", lambda eng: eng.copy(braw[:, :], pbc[0:64, 0:TT]),
                 reads=("pbc",), writes=("braw",))
            p.op("scalar", lambda eng: eng.copy(craw[:, :], pbc[0:64, TT:2 * TT]),
                 reads=("pbc",), writes=("craw",))

            p.op("scalar",
                 lambda eng: eng.activation(r_ab[:, :], prow[0:1, 0:TT], Act.Abs),
                 reads=("prow",), writes=("rab",))
            p.op("scalar",
                 lambda eng: eng.activation(r_en[:, :], r_ab[:, :], Act.Exp,
                                            bias=0.0, scale=-1.0),
                 reads=("rab",), writes=("ren",))
            p.op("scalar",
                 lambda eng: eng.activation(r_ln[:, :], r_en[:, :], Act.Ln,
                                            bias=1.0, scale=1.0),
                 reads=("ren",), writes=("rln",))
            p.op("vector",
                 lambda eng: eng.tensor_scalar(r_mx[:, :], prow[0:1, 0:TT], 0.0,
                                               None, Alu.max),
                 reads=("prow",), writes=("rmx",))
            p.op("vector",
                 lambda eng: eng.tensor_add(r_dl[:, :], r_mx[:, :], r_ln[:, :]),
                 reads=("rmx", "rln"), writes=("rdl",))
            p.op("scalar",
                 lambda eng: eng.activation(r_ar[:, :], r_dl[:, :], Act.Exp,
                                            bias=0.0, scale=an_sb[0:1, 0:1]),
                 reads=("rdl", "aneg"), writes=("rar",))
            p.op("gpsimd",
                 lambda eng: eng.partition_broadcast(abc_sb[:, :], r_ar[0:1, :], 128),
                 reads=("rar",), writes=("abc",))

            for pr in range(8):
                cc0 = 2 * pr
                pair = slice(cc0, cc0 + 2)

                def bmm(c0=cc0):
                    def f(eng):
                        eng.matmul(pB[:, 0:TT], wB_sb[:, c0, :], braw[:, :],
                                   start=True, stop=False)
                        return eng.matmul(pB[:, TT:2 * TT], wB_sb[:, c0 + 1, :],
                                          braw[:, :], start=False, stop=True)
                    return f

                def cmm(c0=cc0):
                    def f(eng):
                        eng.matmul(pC[:, 0:TT], wC_sb[:, c0, :], craw[:, :],
                                   start=True, stop=False)
                        return eng.matmul(pC[:, TT:2 * TT], wC_sb[:, c0 + 1, :],
                                          craw[:, :], start=False, stop=True)
                    return f
                p.op("tensor", bmm(), reads=("wB", "braw", "pB"), writes=("pB",))
                p.op("tensor", cmm(), reads=("wC", "craw", "pC"), writes=("pC",))
                p.op("vector",
                     (lambda c0: lambda eng: eng.tensor_tensor(
                         bb_sb[:, c0:c0 + 2, :].rearrange("p c t -> p (c t)"),
                         pB[:, :],
                         xz_sb[:, c0:c0 + 2, :].rearrange("p c t -> p (c t)"),
                         Alu.mult))(cc0),
                     reads=("pB", "xz"), writes=("bb",))
                for ci in (cc0, cc0 + 1):
                    for sq_ in range(2):
                        t0 = sq_ * 128
                        if kk == 0:
                            init = 0.0
                        else:
                            init = carry[:, 2 * ci + sq_: 2 * ci + sq_ + 1]
                        p.op("vector",
                             (lambda c_, tt0, ini: lambda eng: eng.tensor_tensor_scan(
                                 hh_sb[:, c_, tt0:tt0 + 128],
                                 abc_sb[:, tt0:tt0 + 128],
                                 bb_sb[:, c_, tt0:tt0 + 128],
                                 ini, Alu.mult, Alu.add))(ci, t0, init),
                             reads=("abc", "bb", "carry"), writes=("hh",))
                p.op("vector",
                     (lambda c0: lambda eng: eng.tensor_copy(
                         carry[:, 2 * c0:2 * c0 + 4],
                         hh_sb[:, :, :].rearrange("p c t -> p (c t)")
                         [:, c0 * TT + 127::128][:, 0:4]))(cc0),
                     reads=("hh",), writes=("carry",))
                p.op("vector",
                     (lambda c0: lambda eng: eng.tensor_tensor(
                         bb_sb[:, c0:c0 + 2, :].rearrange("p c t -> p (c t)"),
                         pC[:, :],
                         hh_sb[:, c0:c0 + 2, :].rearrange("p c t -> p (c t)"),
                         Alu.mult))(cc0),
                     reads=("pC", "hh"), writes=("bb",))
                dD_b = dD_sb[:, pair].unsqueeze(2).broadcast_to([128, 2, TT])
                p.op("vector",
                     (lambda pa, db, pr_: lambda eng: eng.tensor_tensor(
                         sq_sb[:, 2 + (pr_ % 2) * 2:4 + (pr_ % 2) * 2, :],
                         xz_sb[:, pa, :], db, Alu.mult))(pair, dD_b, pr),
                     reads=("xz", "dD"), writes=("sq",))
                p.op("vector",
                     (lambda pa, pr_: lambda eng: eng.tensor_tensor(
                         sq_sb[:, 2 + (pr_ % 2) * 2:4 + (pr_ % 2) * 2, :],
                         bb_sb[:, pa, :],
                         sq_sb[:, 2 + (pr_ % 2) * 2:4 + (pr_ % 2) * 2, :],
                         Alu.add))(pair, pr),
                     reads=("bb", "sq"), writes=("sq",))
                p.op("vector",
                     (lambda pa, c0, pr_: lambda eng: eng.tensor_tensor(
                         hh_sb[:, pa, :],
                         sq_sb[:, 2 + (pr_ % 2) * 2:4 + (pr_ % 2) * 2, :],
                         xz_sb[:, CC + c0:CC + c0 + 2, :], Alu.mult))(pair, cc0, pr),
                     reads=("sq", "xz"), writes=("hh",))
                for mg in range(2):
                    for k16 in (cc0, cc0 + 1):
                        li = (pr * 4 + mg * 2 + (k16 - cc0)) % 2
                        buf = wob[li]
                        bn = f"wob{li}"
                        p.dma("sync", bn,
                              (lambda bf, kc, m0: lambda eng: eng.dma_start(
                                  bf[:, :], wo_h[kc][:, m0 * 512:(m0 + 1) * 512]))(
                                  buf, k16, mg),
                              writes=(bn,))

                        def womm(bf=buf, k16_=k16, mg_=mg):
                            def f(eng):
                                last = None
                                for j in range(4):
                                    bank = pm[2 * mg_ + j // 2]
                                    c0_ = (j % 2) * TT
                                    last = eng.matmul(
                                        bank[:, c0_:c0_ + TT],
                                        bf[:, j * 128:(j + 1) * 128],
                                        hh_sb[:, k16_, :],
                                        start=(k16_ == 0 and j % 2 == 0),
                                        stop=(k16_ == CC - 1 and j % 2 == 1))
                                return last
                            return f
                        p.op("tensor", womm(),
                             reads=(bn, "hh", f"pm{2 * mg}", f"pm{2 * mg + 1}"),
                             writes=(f"pm{2 * mg}", f"pm{2 * mg + 1}"))
            for bkn in range(4):
                mglob = 2 * bkn
                p.op("scalar",
                     (lambda b_, mg: lambda eng: eng.copy(
                         mx_sb[:, mg:mg + 2, :], pm[b_][:, :]))(bkn, mglob),
                     reads=(f"pm{bkn}",), writes=("mx",))

        def ln2_u(kk):
            norm_stats(mx_sb, DC, "mx")
            p.op("vector",
                 lambda eng: eng.tensor_tensor(sq3, mx3, bc(t_mm, DC), Alu.subtract),
                 reads=("mx", "mm"), writes=("sq",))
            p.op("vector",
                 lambda eng: eng.tensor_tensor(st3, sq3, bc(t_rr, DC), Alu.mult),
                 reads=("sq", "rr"), writes=("st",))
            p.op("vector",
                 lambda eng: eng.tensor_tensor(x13, st3, x3, Alu.add),
                 reads=("st", "x"), writes=("x1",))

        def ln3_q1(kk):
            norm_stats(x1_sb, DC, "x1")
            p.op("vector",
                 lambda eng: eng.tensor_tensor(sq3, x13, bc(t_mm, DC), Alu.subtract),
                 reads=("x1", "mm"), writes=("sq",))
            p.op("vector",
                 lambda eng: eng.tensor_tensor(mx3, sq3, bc(t_rr, DC), Alu.mult),
                 reads=("sq", "rr"), writes=("mx",))
            p.op("scalar",
                 lambda eng: eng.activation(sq3, mx3, Act.Square),
                 reads=("mx",), writes=("sq",))

            def sq_sums():
                def f(eng):
                    last = None
                    for c in range(DC):
                        last = eng.matmul(prow[0:1, 0:TT], on_sb[:, 0:1],
                                          sq_sb[:, c, :],
                                          start=(c == 0), stop=(c == DC - 1))
                    return last
                return f
            p.op("tensor", sq_sums(), reads=("sq", "ones", "prow"),
                 writes=("prow",))
            p.op("vector",
                 lambda eng: eng.tensor_reduce(
                     t_ms[:, :], sq3.transpose([0, 2, 1]), AX.X, Alu.max),
                 reads=("sq",), writes=("tms",))
            p.op("gpsimd",
                 lambda eng: eng.partition_all_reduce(
                     t_ax[:, :], t_ms[:, :], 128, ReduceOp.max),
                 reads=("tms",), writes=("ax",))
            p.op("scalar",
                 lambda eng: eng.activation(r_rs[:, :], prow[0:1, 0:TT], Act.Ln,
                                            bias=eps_sb[0:1, 0:1], scale=1.0 / D),
                 reads=("prow", "eps"), writes=("rrs",))
            p.op("scalar",
                 lambda eng: eng.activation(r_rs[:, :], r_rs[:, :], Act.Exp,
                                            bias=0.0, scale=-0.5),
                 reads=("rrs",), writes=("rrs",))
            p.op("gpsimd",
                 lambda eng: eng.partition_broadcast(t_rr[:, :], r_rs[0:1, :], 128),
                 reads=("rrs",), writes=("rr",))
            p.op("scalar",
                 lambda eng: eng.activation(t_ms[:, :], t_ax[:, :], Act.Ln),
                 reads=("ax",), writes=("tms",))
            p.op("scalar",
                 lambda eng: eng.activation(t_ax[:, :], t_ms[:, :], Act.Exp,
                                            bias=0.0, scale=0.5),
                 reads=("tms",), writes=("ax",))
            p.op("vector",
                 lambda eng: eng.tensor_mul(t_ax[:, :], t_ax[:, :], t_rr[:, :]),
                 reads=("ax", "rr"), writes=("ax",))
            p.op("vector",
                 lambda eng: eng.tensor_scalar(t_ax[:, :], t_ax[:, :], 1e-5, None,
                                               Alu.max),
                 reads=("ax",), writes=("ax",))
            p.op("vector", lambda eng: eng.reciprocal(t_rc[:, :], t_ax[:, :]),
                 reads=("ax",), writes=("rc",))
            p.op("vector",
                 lambda eng: eng.scalar_tensor_tensor(
                     t_sc[:, :], t_rr[:, :], 127.0, t_rc[:, :], Alu.mult, Alu.mult),
                 reads=("rr", "rc"), writes=("sc",))
            p.op("vector",
                 lambda eng: eng.tensor_scalar(t_fs[:, :], t_ax[:, :], m1c[:, 0:1],
                                               1.0 / 127.0, Alu.mult, Alu.mult),
                 reads=("ax", "m1c"), writes=("fs",))
            p.op("vector",
                 lambda eng: eng.tensor_tensor(hh_sb[:, 0:DC, :], mx3, bc(t_sc, DC),
                                               Alu.mult),
                 reads=("mx", "sc"), writes=("hh",))
            p.op("scalar",
                 lambda eng: eng.activation(sq3, hh_sb[:, 0:DC, :], Act.Copy,
                                            bias=C32, scale=1.0),
                 reads=("hh",), writes=("sq",))
            p.op("scalar",
                 lambda eng: eng.activation(st3, hh_sb[:, 0:DC, :], Act.Copy,
                                            bias=C64, scale=1.0),
                 reads=("hh",), writes=("st",))
            p.op("vector",
                 lambda eng: eng.scalar_tensor_tensor(
                     st3, st3, -C64, hh_sb[:, 0:DC, :], Alu.add, Alu.subtract),
                 reads=("st", "hh"), writes=("st",))
            p.op("vector",
                 lambda eng: eng.scalar_tensor_tensor(
                     st3, st3, fF[:, 0:1], sq3, Alu.mult, Alu.add),
                 reads=("st", "fF", "sq"), writes=("st",))
            p.op("vector",
                 lambda eng: eng.tensor_scalar(q1_sb[:, :, :], st3, -C32, None,
                                               Alu.add),
                 reads=("st",), writes=("q1",))

        def ffn1_stage(kk):
            for jg in range(4):
                for k8 in range(DC):
                    li = jg * DC + k8
                    buf = wq1b[li % 2]
                    bn = f"wq1b{li % 2}"
                    p.dma("sync", bn,
                          (lambda bf, kc, j0: lambda eng: eng.dma_start(
                              bf[:, :], wq1_h[kc][:, j0 * 1024:(j0 + 1) * 1024]))(
                              buf, k8, jg),
                          writes=(bn,))

                    def f1mm(bf=buf, k8_=k8):
                        def f(eng):
                            last = None
                            for j in range(8):
                                bank = pm[j // 2]
                                c0 = (j % 2) * TT
                                last = eng.matmul(
                                    bank[:, c0:c0 + TT],
                                    bf[:, j * 128:(j + 1) * 128],
                                    q1_sb[:, k8_, :],
                                    start=(k8_ == 0 and j % 2 == 0),
                                    stop=(k8_ == DC - 1 and j % 2 == 1))
                            return last
                        return f
                    p.op("tensor", f1mm(),
                         reads=(bn, "q1", "pm0", "pm1", "pm2", "pm3"),
                         writes=("pm0", "pm1", "pm2", "pm3"))
                for bkn in range(4):
                    mglob = jg * 8 + 2 * bkn
                    fs_b1 = t_fs[:, :].unsqueeze(1).broadcast_to([128, 1, TT])
                    for half in range(2):
                        mg1 = mglob + half
                        slot = sq_sb[:, 2 * (bkn % 2) + half, :]
                        p.op("vector",
                             (lambda bk, sl, hf: lambda eng:
                              eng.scalar_tensor_tensor(
                                  sl, pm[bk][:, hf * TT:(hf + 1) * TT], 0.0,
                                  t_fs[:, :], Alu.bypass, Alu.mult))(
                                  bkn, slot, half),
                             reads=(f"pm{bkn}", "fs"), writes=("sq",))
                        p.op("scalar",
                             (lambda sl, mg: lambda eng: eng.activation(
                                 xz_sb[:, mg, :], sl, Act.Gelu,
                                 bias=b1_sb[:, mg:mg + 1], scale=1.0))(slot, mg1),
                             reads=("sq", "b1c"), writes=("xz",))

        def stats2_q2(kk):
            for pc in range(4):
                gsl = xz_sb[:, pc * 8:(pc + 1) * 8, :]
                p.op("scalar",
                     (lambda gs: lambda eng: eng.activation(
                         sq3, gs, Act.Square))(gsl),
                     reads=("xz",), writes=("sq",))

                def sq_sums2(pc_=pc):
                    def f(eng):
                        last = None
                        for c in range(DC):
                            last = eng.matmul(
                                prow[0:1, 0:TT], on_sb[:, 0:1], sq_sb[:, c, :],
                                start=(pc_ == 0 and c == 0),
                                stop=(pc_ == 3 and c == DC - 1))
                        return last
                    return f
                p.op("tensor", sq_sums2(), reads=("sq", "ones", "prow"),
                     writes=("prow",))
                p.op("vector",
                     lambda eng: eng.tensor_reduce(
                         t_ms[:, :], sq3.transpose([0, 2, 1]), AX.X, Alu.max),
                     reads=("sq",), writes=("tms",))
                if pc == 0:
                    p.op("vector",
                         lambda eng: eng.tensor_copy(t_am[:, :], t_ms[:, :]),
                         reads=("tms",), writes=("am",))
                else:
                    p.op("vector",
                         lambda eng: eng.tensor_tensor(t_am[:, :], t_am[:, :],
                                                       t_ms[:, :], Alu.max),
                         reads=("tms", "am"), writes=("am",))
            p.op("gpsimd",
                 lambda eng: eng.partition_all_reduce(
                     t_ax[:, :], t_am[:, :], 128, ReduceOp.max),
                 reads=("am",), writes=("ax",))
            p.op("scalar",
                 lambda eng: eng.activation(r_rs[:, :], prow[0:1, 0:TT], Act.Ln,
                                            bias=eps_sb[0:1, 0:1], scale=1.0 / DF),
                 reads=("prow", "eps"), writes=("rrs",))
            p.op("scalar",
                 lambda eng: eng.activation(r_rs[:, :], r_rs[:, :], Act.Exp,
                                            bias=0.0, scale=-0.5),
                 reads=("rrs",), writes=("rrs",))
            p.op("gpsimd",
                 lambda eng: eng.partition_broadcast(t_rr[:, :], r_rs[0:1, :], 128),
                 reads=("rrs",), writes=("rr",))
            p.op("scalar",
                 lambda eng: eng.activation(t_ms[:, :], t_ax[:, :], Act.Ln),
                 reads=("ax",), writes=("tms",))
            p.op("scalar",
                 lambda eng: eng.activation(t_ax[:, :], t_ms[:, :], Act.Exp,
                                            bias=0.0, scale=0.5),
                 reads=("tms",), writes=("ax",))
            p.op("vector",
                 lambda eng: eng.tensor_mul(t_ax[:, :], t_ax[:, :], t_rr[:, :]),
                 reads=("ax", "rr"), writes=("ax",))
            p.op("vector",
                 lambda eng: eng.tensor_scalar(t_ax[:, :], t_ax[:, :], 1e-5, None,
                                               Alu.max),
                 reads=("ax",), writes=("ax",))
            p.op("vector", lambda eng: eng.reciprocal(t_rc[:, :], t_ax[:, :]),
                 reads=("ax",), writes=("rc",))
            p.op("vector",
                 lambda eng: eng.scalar_tensor_tensor(
                     t_sc[:, :], t_rr[:, :], 127.0, t_rc[:, :], Alu.mult, Alu.mult),
                 reads=("rr", "rc"), writes=("sc",))
            p.op("vector",
                 lambda eng: eng.tensor_scalar(t_fs[:, :], t_ax[:, :], m2c[:, 0:1],
                                               1.0 / 127.0, Alu.mult, Alu.mult),
                 reads=("ax", "m2c"), writes=("fs",))
        def q2_ffn2(kk):
            for pc in range(4):
                gsl = xz_sb[:, pc * 8:(pc + 1) * 8, :]
                p.op("vector",
                     (lambda gs: lambda eng: eng.tensor_tensor(
                         mx3, gs, bc(t_sc, DC), Alu.mult))(gsl),
                     reads=("xz", "sc"), writes=("mx",))
                p.op("scalar",
                     lambda eng: eng.activation(sq3, mx3, Act.Copy,
                                                bias=C32, scale=1.0),
                     reads=("mx",), writes=("sq",))
                p.op("scalar",
                     lambda eng: eng.activation(st3, mx3, Act.Copy,
                                                bias=C64, scale=1.0),
                     reads=("mx",), writes=("st",))
                p.op("vector",
                     lambda eng: eng.scalar_tensor_tensor(
                         st3, st3, -C64, mx3, Alu.add, Alu.subtract),
                     reads=("st", "mx"), writes=("st",))
                p.op("vector",
                     lambda eng: eng.scalar_tensor_tensor(
                         st3, st3, fF[:, 0:1], sq3, Alu.mult, Alu.add),
                     reads=("st", "fF", "sq"), writes=("st",))
                p.op("vector",
                     (lambda p_: lambda eng: eng.tensor_scalar(
                         q2_sb[:, p_ * 8:(p_ + 1) * 8, :], st3, -C32, None,
                         Alu.add))(pc),
                     reads=("st",), writes=("q2",))
                for k32 in range(pc * 8, (pc + 1) * 8):
                    buf = wq2b[k32 % 2]
                    bn = f"wq2b{k32 % 2}"
                    p.dma("sync", bn,
                          (lambda bf, kc: lambda eng: eng.dma_start(
                              bf[:, :], wq2_h[kc]))(buf, k32),
                          writes=(bn,))

                    def f2mm(bf=buf, k32_=k32):
                        def f(eng):
                            last = None
                            for j in range(8):
                                bank = pm[j // 2]
                                c0 = (j % 2) * TT
                                last = eng.matmul(
                                    bank[:, c0:c0 + TT],
                                    bf[:, j * 128:(j + 1) * 128],
                                    q2_sb[:, k32_, :],
                                    start=(k32_ == 0 and j % 2 == 0),
                                    stop=(k32_ == FC - 1 and j % 2 == 1))
                            return last
                        return f
                    p.op("tensor", f2mm(),
                         reads=(bn, "q2", "pm0", "pm1", "pm2", "pm3"),
                         writes=("pm0", "pm1", "pm2", "pm3"))

        def ffn2_evac(kk):
            for bkn in range(4):
                mglob = 2 * bkn
                fs_b = t_fs[:, :].unsqueeze(1).broadcast_to([128, 2, TT])
                p.op("vector",
                     (lambda bk, mg: lambda eng: eng.scalar_tensor_tensor(
                         mx_sb[:, mg:mg + 2, :],
                         pm[bk][:, :].rearrange("p (c t) -> p c t", c=2), 0.0,
                         fs_b,
                         Alu.bypass, Alu.mult))(bkn, mglob),
                     reads=(f"pm{bkn}", "fs"), writes=("mx",))
            b2_b = b2_sb[:, :].unsqueeze(2).broadcast_to([128, DC, TT])
            p.op("vector",
                 lambda eng: eng.tensor_tensor(mx3, mx3, b2_b, Alu.add),
                 reads=("mx", "b2c"), writes=("mx",))

        def ln4_out(kk):
            norm_stats(mx_sb, DC, "mx")
            p.op("vector",
                 lambda eng: eng.tensor_tensor(sq3, mx3, bc(t_mm, DC), Alu.subtract),
                 reads=("mx", "mm"), writes=("sq",))
            p.op("vector",
                 lambda eng: eng.tensor_tensor(st3, sq3, bc(t_rr, DC), Alu.mult),
                 reads=("sq", "rr"), writes=("st",))
            p.op("vector",
                 lambda eng: eng.tensor_tensor(ot3, st3, x13, Alu.add),
                 reads=("st", "x1"), writes=("ot",))
            for b_ in range(2):
                p.dma("scalar", f"out{b_}",
                      (lambda b2_, kt: lambda eng: eng.dma_start(
                          o_h[b2_, kt], ot_sb[:, :, b2_ * 128:(b2_ + 1) * 128]))(b_, kk),
                      reads=("ot",), writes=())

        x_load(0)
        ln1_front(0)
        win_stage(0)
        for k in range(n_tiles):
            mixer_mid(k)
            ln2_u(k)
            if k + 1 < n_tiles:
                x_load(k + 1)
            ln3_q1(k)
            if k + 1 < n_tiles:
                ln1_front(k + 1)
            ffn1_stage(k)
            stats2_q2(k)
            q2_ffn2(k)
            ffn2_evac(k)
            if k + 1 < n_tiles:
                win_stage(k + 1)
            ln4_out(k)

        p.final_wait("sync", "out0")
        p.final_wait("sync", "out1")

        _emit(nc, p, esems, dsems)

    m = {}
    for lib in all_libraries:
        for t in lib.instructions:
            m[t] = m.get(t, 0) | (1 << lib.index)
    _bass_rust.insert_library_loads(nc, m, len(all_libraries), standard.index)
    _bass_rust.insert_act_table_loads(nc)
    mybir.codegen_inst_isa_subclasses(nc)
    return nc


def make_in_map(core_id, inputs):
    x = np.asarray(inputs["L0_attn_x"])
    W_in = np.ascontiguousarray(np.asarray(inputs["L0_attn_W_in"], np.float32))
    W_xp = np.asarray(inputs["L0_attn_W_xp"])
    W_B = np.asarray(inputs["L0_attn_W_B"])
    W_C = np.asarray(inputs["L0_attn_W_C"])
    W_out = np.asarray(inputs["L0_attn_W_out"])
    Dv = np.asarray(inputs["L0_attn_D"])
    A_log = np.asarray(inputs["L0_attn_A_log"])
    w1 = np.asarray(inputs["L0_ffn_w1"])
    b1 = np.asarray(inputs["L0_ffn_b1"])
    w2 = np.asarray(inputs["L0_ffn_w2"])
    b2 = np.asarray(inputs["L0_ffn_b2"])

    xc = x[2 * core_id:2 * core_id + 2]
    xt = xc.transpose(0, 2, 1).reshape(2, DC, 128, NT, 128)
    xt = np.ascontiguousarray(xt.transpose(0, 3, 2, 1, 4)).astype(np.float32)

    return {
        "x": xt,
        "w_in": np.ascontiguousarray(W_in.reshape(DC, 128, 2 * DI), np.float32),
        "w_xd": np.ascontiguousarray(
            W_xp[:, 0].reshape(CC, 128).T, np.float32),
        "w_xbc": np.ascontiguousarray(
            W_xp[:, 1:129].reshape(CC, 128, 128).transpose(1, 0, 2), np.float32),
        "w_B": np.ascontiguousarray(
            W_B.reshape(64, CC, 128), np.float32),
        "w_C": np.ascontiguousarray(
            W_C.reshape(64, CC, 128), np.float32),
        "w_out": np.ascontiguousarray(W_out.reshape(CC, 128, D), np.float32),
        "d_vec": np.ascontiguousarray(Dv.reshape(CC, 128).T, np.float32),
        "a_log": np.asarray(A_log, np.float32).reshape(1, 1),
        "w1t": np.ascontiguousarray(w1.T.reshape(DC, 128, DF), np.float32),
        "b1c": np.ascontiguousarray(b1.reshape(FC, 128).T, np.float32),
        "w2t": np.ascontiguousarray(w2.T.reshape(FC, 128, D), np.float32),
        "b2c": np.ascontiguousarray(b2.reshape(DC, 128).T, np.float32),
    }


def assemble_out(res_list):
    outs = []
    for r in res_list:
        o = np.asarray(r["out"])
        o = o.transpose(0, 3, 2, 1, 4).reshape(2, D, 2048).transpose(0, 2, 1)
        outs.append(o)
    return np.ascontiguousarray(np.concatenate(outs, 0), dtype=np.float32)


def kernel(**inputs):
    nc = build_program()
    in_maps = [make_in_map(c, inputs) for c in range(8)]
    res = run_bass_kernel_spmd(nc, in_maps, core_ids=list(range(8)))
    return assemble_out(res.results)


# version 12
# speedup vs baseline: 1.0982x; 1.0239x over previous
import numpy as np

import bass_rust as _bass_rust
import concourse.bass as bass
import concourse.mybir as mybir
from concourse.bass_isa import ReduceOp
from concourse.bass_utils import run_bass_kernel_spmd
from concourse.library_config import all_libraries, standard

F32 = mybir.dt.float32
F16 = mybir.dt.float16
Alu = mybir.AluOpType
Act = mybir.ActivationFunctionType
AX = mybir.AxisListType

C32 = 12582912.0
C64 = 6755399441055744.0

NT = 16
TT = 256
D, DI, DF = 1024, 2048, 4096
DC, CC, FC = 8, 16, 32

ENGINES = ("tensor", "vector", "scalar", "gpsimd", "sync")


class B:

    def __init__(self):
        self.nodes = []
        self.ver = {}
        self.readers = {}

    def _mk(self, engine, kind, fn, reads, writes, stream, cost):
        nid = len(self.nodes)
        deps = set()
        for b in reads:
            v = self.ver.get(b)
            if v is not None:
                deps.add(v)
        for b in writes:
            v = self.ver.get(b)
            if v is not None:
                deps.add(v)
            deps.update(self.readers.get(b, ()))
        deps.discard(nid)
        self.nodes.append(dict(id=nid, engine=engine, kind=kind, fn=fn,
                               stream=stream, deps=deps, reads=tuple(reads),
                               writes=tuple(writes), cost=cost))
        for b in writes:
            self.ver[b] = nid
            self.readers[b] = set()
        for b in reads:
            self.readers.setdefault(b, set()).add(nid)
        return nid

    def op(self, engine, fn, reads=(), writes=(), cost=None):
        if cost is None:
            cost = {"tensor": 2000, "vector": 1500, "scalar": 1700,
                    "gpsimd": 3000}.get(engine, 1500)
        self._mk(engine, "op", fn, reads, writes, None, cost)

    def dma(self, engine, stream, fn, reads=(), writes=()):
        self._mk(engine, "dma", fn, reads, writes, stream, 700)

    def final_wait(self, engine, stream):
        nid = self._mk(engine, "finalwait", None, (), (), stream, 10)
        self.nodes[nid]["deps"] = {n["id"] for n in self.nodes
                                   if n["kind"] == "dma" and n["stream"] == stream}

    def schedule(self):
        import heapq
        n = len(self.nodes)
        ndep = [len(nd["deps"]) for nd in self.nodes]
        children = [[] for _ in range(n)]
        for nd in self.nodes:
            for d in nd["deps"]:
                children[nd["id"]].append(d)
        dependents = [[] for _ in range(n)]
        for nd in self.nodes:
            for d in nd["deps"]:
                dependents[d].append(nd["id"])
        ready_t = [0.0] * n
        engines = {}
        for nd in self.nodes:
            engines.setdefault(nd["engine"], [])
        eng_free = {e: 0.0 for e in engines}
        ready = {e: [] for e in engines}
        for nd in self.nodes:
            if ndep[nd["id"]] == 0:
                heapq.heappush(ready[nd["engine"]], nd["id"])
        done_t = [0.0] * n
        order = {e: [] for e in engines}
        import itertools
        remaining = n
        SEM_LAT = 150.0
        while remaining:
            best = None
            for e, h in ready.items():
                if not h:
                    continue
                nid = h[0]
                t = max(eng_free[e], ready_t[nid])
                if best is None or t < best[0]:
                    best = (t, e, nid)
            if best is None:
                raise RuntimeError("scheduler deadlock (cyclic deps?)")
            t, e, nid = best
            heapq.heappop(ready[e])
            nd = self.nodes[nid]
            dur = nd["cost"]
            t_end = t + dur
            eng_free[e] = t_end
            done_t[nid] = t_end
            order[e].append(nid)
            remaining -= 1
            for c in dependents[nid]:
                ndep[c] -= 1
                ready_t[c] = max(ready_t[c], t_end + SEM_LAT)
                if ndep[c] == 0:
                    heapq.heappush(ready[self.nodes[c]["engine"]], c)
        self.order = order
        return order


def _emit(nc, plan, esems, dsems):
    order = plan.schedule()
    nodes = plan.nodes
    pos = {}
    for e, lst in order.items():
        cnt = 0
        for nid in lst:
            if nodes[nid]["kind"] == "op":
                cnt += 1
            pos[nid] = cnt
    dcount = {}
    dma_val = {}
    for e, lst in order.items():
        pass
    for e, lst in order.items():
        for nid in lst:
            nd = nodes[nid]
            if nd["kind"] == "dma":
                dcount[nd["stream"]] = dcount.get(nd["stream"], 0) + 16
                dma_val[nid] = dcount[nd["stream"]]

    with nc.Block() as block:
        def section(ename):
            def run(eng):
                lst = order.get(ename, [])
                last_wait = {}
                dirty = set()
                for nid in lst:
                    nd = nodes[nid]
                    if nd["kind"] == "finalwait":
                        eng.wait_ge(dsems[nd["stream"]],
                                    dcount.get(nd["stream"], 0))
                        continue
                    waits = {}
                    drain = bool(set(nd["reads"]) & dirty)
                    for d in nd["deps"]:
                        dn = nodes[d]
                        if dn["engine"] == ename:
                            if dn["kind"] != "dma":
                                continue
                            sk = ("dma", dn["stream"])
                            val = dma_val[d]
                        elif dn["kind"] == "dma":
                            sk = ("dma", dn["stream"])
                            val = dma_val[d]
                        else:
                            sk = ("eng", dn["engine"])
                            val = pos[d]
                        if waits.get(sk, -1) < val:
                            waits[sk] = val
                    if drain:
                        if ename != "tensor":
                            eng.drain()
                        dirty.clear()
                    for sk, val in sorted(waits.items()):
                        if last_wait.get(sk, -1) < val:
                            last_wait[sk] = val
                            sem = esems[sk[1]] if sk[0] == "eng" else dsems[sk[1]]
                            eng.wait_ge(sem, val)
                    ins = nd["fn"](eng)
                    if nd["kind"] == "dma":
                        ins.then_inc(dsems[nd["stream"]], 16)
                    else:
                        ins.then_inc(esems[ename], 1)
                        dirty.update(nd["writes"])
            return run

        block.tensor(section("tensor"))
        block.vector(section("vector"))
        block.scalar(section("scalar"))
        block.gpsimd(section("gpsimd"))
        block.sync(section("sync"))


def build_program(n_tiles=NT, debug=False):
    nc = bass.Bass("TRN2", target_bir_lowering=False,
                   detect_race_conditions=False)

    x_h = nc.dram_tensor("x", [2, NT, 128, DC, 128], F32, kind="ExternalInput")
    o_h = nc.dram_tensor("out", [2, NT, 128, DC, 128], F32, kind="ExternalOutput")
    win_h = nc.dram_tensor("w_in", [DC, 128, 2 * DI], F32, kind="ExternalInput")
    wd_h = nc.dram_tensor("w_xd", [128, CC], F32, kind="ExternalInput")
    wbc_h = nc.dram_tensor("w_xbc", [128, CC, 128], F32, kind="ExternalInput")
    wB_h = nc.dram_tensor("w_B", [64, CC, 128], F32, kind="ExternalInput")
    wC_h = nc.dram_tensor("w_C", [64, CC, 128], F32, kind="ExternalInput")
    wo_h = nc.dram_tensor("w_out", [CC, 128, D], F32, kind="ExternalInput")
    dD_h = nc.dram_tensor("d_vec", [128, CC], F32, kind="ExternalInput")
    al_h = nc.dram_tensor("a_log", [1, 1], F32, kind="ExternalInput")
    w1_h = nc.dram_tensor("w1t", [DC, 128, DF], F32, kind="ExternalInput")
    b1_h = nc.dram_tensor("b1c", [128, FC], F32, kind="ExternalInput")
    w2_h = nc.dram_tensor("w2t", [FC, 128, D], F32, kind="ExternalInput")
    b2_h = nc.dram_tensor("b2c", [128, DC], F32, kind="ExternalInput")
    wq1_h = nc.dram_tensor("wq1d", [DC, 128, DF], F16, kind="Internal")
    wq2_h = nc.dram_tensor("wq2d", [FC, 128, D], F16, kind="Internal")

    dbg = {}
    if debug:
        for nm, sh in [("d_x1", [128, DC, TT]), ("d_xi", [128, CC, TT]),
                       ("d_s", [128, CC, TT]), ("d_h", [128, CC, TT]),
                       ("d_y2", [128, CC, TT]), ("d_u", [128, DC, TT]),
                       ("d_q1", [128, DC, TT]), ("d_g", [128, FC, TT]),
                       ("d_abar", [1, TT]), ("d_mix", [128, DC, TT])]:
            dbg[nm] = nc.dram_tensor(nm, sh, F32, kind="ExternalOutput")

    from contextlib import ExitStack
    stack = ExitStack()
    e = stack.enter_context
    with stack:
        x_sb = e(nc.sbuf_tensor("x_sb", [128, DC, TT], F32))
        x1_sb = e(nc.sbuf_tensor("x1_sb", [128, DC, TT], F32))
        sq_sb = e(nc.sbuf_tensor("sq_sb", [128, DC, TT], F32))
        st_sb = e(nc.sbuf_tensor("st_sb", [128, DC, TT], F32))
        mx_sb = e(nc.sbuf_tensor("mx_sb", [128, DC, TT], F32))
        xz_sb = e(nc.sbuf_tensor("xz_sb", [128, 2 * CC, TT], F32))
        bb_sb = e(nc.sbuf_tensor("bb_sb", [128, CC, TT], F32))
        ot_sb = e(nc.sbuf_tensor("ot_sb", [128, DC, TT], F32))
        hh_sb = e(nc.sbuf_tensor("hh_sb", [128, CC, TT], F32))
        q1_sb = e(nc.sbuf_tensor("q1_sb", [128, DC, TT], F16))
        q2_sb = e(nc.sbuf_tensor("q2_sb", [128, FC, TT], F16))
        carry = e(nc.sbuf_tensor("carry", [128, 32], F32))
        braw = e(nc.sbuf_tensor("braw", [64, TT], F32))
        craw = e(nc.sbuf_tensor("craw", [64, TT], F32))
        abc_sb = e(nc.sbuf_tensor("abc_sb", [128, TT], F32))
        t_mm = e(nc.sbuf_tensor("t_mm", [128, TT], F32))
        t_rr = e(nc.sbuf_tensor("t_rr", [128, TT], F32))
        t_ms = e(nc.sbuf_tensor("t_ms", [128, TT], F32))
        t_ax = e(nc.sbuf_tensor("t_ax", [128, TT], F32))
        t_sc = e(nc.sbuf_tensor("t_sc", [128, TT], F32))
        t_fs = e(nc.sbuf_tensor("t_fs", [128, TT], F32))
        t_aq = e(nc.sbuf_tensor("t_aq", [128, TT], F32))
        t_am = e(nc.sbuf_tensor("t_am", [128, TT], F32))
        t_rc = e(nc.sbuf_tensor("t_rc", [128, TT], F32))
        r_ab = e(nc.sbuf_tensor("r_ab", [1, TT], F32))
        r_en = e(nc.sbuf_tensor("r_en", [1, TT], F32))
        r_ln = e(nc.sbuf_tensor("r_ln", [1, TT], F32))
        r_mx = e(nc.sbuf_tensor("r_mx", [1, TT], F32))
        r_dl = e(nc.sbuf_tensor("r_dl", [1, TT], F32))
        r_ar = e(nc.sbuf_tensor("r_ar", [1, TT], F32))
        wd_sb = e(nc.sbuf_tensor("wd_sb", [128, CC], F32))
        wbc_sb = e(nc.sbuf_tensor("wbc_sb", [128, CC, 128], F32))
        wB_sb = e(nc.sbuf_tensor("wB_sb", [64, CC, 128], F32))
        wC_sb = e(nc.sbuf_tensor("wC_sb", [64, CC, 128], F32))
        dD_sb = e(nc.sbuf_tensor("dD_sb", [128, CC], F32))
        b1_sb = e(nc.sbuf_tensor("b1_sb", [128, FC], F32))
        b2_sb = e(nc.sbuf_tensor("b2_sb", [128, DC], F32))
        al_sb = e(nc.sbuf_tensor("al_sb", [1, 1], F32))
        an_sb = e(nc.sbuf_tensor("an_sb", [1, 1], F32))
        fF = e(nc.sbuf_tensor("fF", [128, 1], F32))
        eps_sb = e(nc.sbuf_tensor("eps_sb", [128, 1], F32))
        on_sb = e(nc.sbuf_tensor("on_sb", [128, 1], F32))
        r_sm = e(nc.sbuf_tensor("r_sm", [1, TT], F32))
        r_tmp = e(nc.sbuf_tensor("r_tmp", [1, TT], F32))
        r_rs = e(nc.sbuf_tensor("r_rs", [1, TT], F32))
        m1c = e(nc.sbuf_tensor("m1c", [128, 1], F32))
        m2c = e(nc.sbuf_tensor("m2c", [128, 1], F32))
        sw1 = e(nc.sbuf_tensor("sw1", [128, 1], F32))
        sw2 = e(nc.sbuf_tensor("sw2", [128, 1], F32))
        macc = e(nc.sbuf_tensor("macc", [128, 48], F32))
        winb = [e(nc.sbuf_tensor(f"winb{i}", [128, 1024], F32)) for i in (0, 1)]
        wob = [e(nc.sbuf_tensor(f"wob{i}", [128, 512], F32)) for i in (0, 1)]
        wq1b = [e(nc.sbuf_tensor(f"wq1b{i}", [128, 1024], F16)) for i in (0, 1)]
        wq2b = [e(nc.sbuf_tensor(f"wq2b{i}", [128, 1024], F16)) for i in (0, 1)]
        pm = [e(nc.psum_tensor(f"pm{i}", [128, 512], F32)) for i in range(4)]
        prow = e(nc.psum_tensor("prow", [1, 2 * TT], F32))
        pbc = e(nc.psum_tensor("pbc", [64, 512], F32))
        pB = e(nc.psum_tensor("pB", [128, 512], F32))
        pC = e(nc.psum_tensor("pC", [128, 512], F32))
        esems = {en: e(nc.semaphore(f"se_{en}")) for en in ENGINES}
        dstreams = (["x0", "x1", "out0", "out1", "xzld", "wqs0", "wqs1"]
                    + ["winb0", "winb1", "wob0", "wob1", "wq1b0", "wq1b1",
                       "wq2b0", "wq2b1"]
                    + ["r_wd", "r_wbc", "r_wB", "r_wC", "r_dD", "r_b1c",
                       "r_b2c", "r_alog"])
        dsems = {s: e(nc.semaphore(f"sd_{s}")) for s in dstreams}

        p = B()

        def bc(tile_ap, nch):
            return tile_ap[:, :].unsqueeze(1).broadcast_to([128, nch, TT])

        for (dst, src, bufn, strm) in [
            (wd_sb, wd_h, "wd", "r_wd"), (wbc_sb, wbc_h, "wbc", "r_wbc"),
            (wB_sb, wB_h, "wB", "r_wB"), (wC_sb, wC_h, "wC", "r_wC"),
            (dD_sb, dD_h, "dD", "r_dD"), (b1_sb, b1_h, "b1c", "r_b1c"),
            (b2_sb, b2_h, "b2c", "r_b2c"), (al_sb, al_h, "alog", "r_alog"),
        ]:
            p.dma("sync", strm,
                  (lambda d, s: lambda eng: eng.dma_start(d[...], s[...]))(dst, src),
                  writes=(bufn,))
        p.op("vector", lambda eng: eng.memset(fF[:, :], 1.0), writes=("fF",))
        p.op("vector",
             lambda eng: eng.tensor_scalar(fF[:, :], fF[:, :], C64, -C64, Alu.add, Alu.add),
             reads=("fF",), writes=("fF",))
        p.op("scalar",
             lambda eng: eng.activation(an_sb[:, :], al_sb[:, :], Act.Exp),
             reads=("alog",), writes=("aneg",))
        p.op("vector",
             lambda eng: eng.tensor_scalar(an_sb[:, :], an_sb[:, :], -1.0, None, Alu.mult),
             reads=("aneg",), writes=("aneg",))
        p.op("vector", lambda eng: eng.memset(carry[:, :], 0.0), writes=("carry",))
        p.op("vector", lambda eng: eng.memset(eps_sb[:, :], 1e-5), writes=("eps",))
        p.op("vector", lambda eng: eng.memset(on_sb[:, :], 1.0), writes=("ones",))

        xzf = xz_sb[:, :, :].rearrange("p c t -> p (c t)")
        q2f = q2_sb[:, :, :].rearrange("p c t -> p (c t)")

        def wmean(w_h, nch, wcols, mcol0, mname):
            for ch in range(nch):
                half = ch % 2
                p.dma("sync", "xzld",
                      (lambda hf, c: lambda eng: eng.dma_start(
                          xzf[:, hf * 4096:hf * 4096 + wcols], w_h[c]))(half, ch),
                      writes=("xz",))
                p.op("vector",
                     (lambda hf, c: lambda eng: eng.tensor_reduce(
                         macc[:, mcol0 + c:mcol0 + c + 1],
                         xzf[:, hf * 4096:hf * 4096 + wcols],
                         AX.X, Alu.max if False else Alu.add,
                         apply_absolute_value=True))(half, ch),
                     reads=("xz",), writes=(mname,))
            p.op("vector",
                 lambda eng: eng.tensor_reduce(
                     macc[:, mcol0:mcol0 + 1], macc[:, mcol0:mcol0 + nch],
                     AX.X, Alu.add),
                 reads=(mname,), writes=(mname,))
            p.op("gpsimd",
                 lambda eng: eng.partition_all_reduce(
                     macc[:, mcol0:mcol0 + 1], macc[:, mcol0:mcol0 + 1],
                     128, ReduceOp.add),
                 reads=(mname,), writes=(mname,))

        wmean(w1_h, DC, DF, 0, "macc1")
        wmean(w2_h, FC, D, 8, "macc2")
        p.op("vector",
             lambda eng: eng.tensor_scalar(m1c[:, :], macc[:, 0:1],
                                           1.0 / (DF * D), 1e-5, Alu.mult, Alu.max),
             reads=("macc1",), writes=("m1c",))
        p.op("vector", lambda eng: eng.reciprocal(sw1[:, :], m1c[:, :]),
             reads=("m1c",), writes=("sw1",))
        p.op("vector",
             lambda eng: eng.tensor_scalar(m2c[:, :], macc[:, 8:9],
                                           1.0 / (DF * D), 1e-5, Alu.mult, Alu.max),
             reads=("macc2",), writes=("m2c",))
        p.op("vector", lambda eng: eng.reciprocal(sw2[:, :], m2c[:, :]),
             reads=("m2c",), writes=("sw2",))

        sqf = sq_sb[:, :, :].rearrange("p c t -> p (c t)")
        stf = st_sb[:, :, :].rearrange("p c t -> p (c t)")
        mxf = mx_sb[:, :, :].rearrange("p c t -> p (c t)")

        def wquant(w_h, wq_hh, nch, wcols, swt, swn, scrname):
            npieces = (wcols + 2047) // 2048
            for ch in range(nch):
                half = ch % 2
                xsl = xzf[:, half * 4096: half * 4096 + wcols]
                qsl = q2f[:, half * 4096: half * 4096 + wcols]
                p.dma("sync", "xzld",
                      (lambda xs, c: lambda eng: eng.dma_start(xs, w_h[c]))(xsl, ch),
                      writes=("xz",))
                for pc in range(npieces):
                    c0 = pc * 2048
                    cw = min(2048, wcols - c0)
                    wsl = xsl[:, c0:c0 + cw]
                    p.op("vector",
                         (lambda ws, cw_: lambda eng: eng.tensor_scalar(
                             sqf[:, :cw_], ws, swt[:, 0:1], None,
                             Alu.mult))(wsl, cw),
                         reads=("xz", swn), writes=("sq",))
                    p.op("scalar",
                         (lambda cw_: lambda eng: eng.activation(
                             stf[:, :cw_], sqf[:, :cw_], Act.Abs))(cw),
                         reads=("sq",), writes=("st",))
                    p.op("vector",
                         (lambda cw_: lambda eng: eng.tensor_scalar(
                             stf[:, :cw_], stf[:, :cw_], 0.5, None,
                             Alu.is_gt))(cw),
                         reads=("st",), writes=("st",))
                    p.op("scalar",
                         (lambda ws, cw_: lambda eng: eng.activation(
                             mxf[:, :cw_], ws, Act.Sign))(wsl, cw),
                         reads=("xz",), writes=("mx",))
                    p.op("vector",
                         (lambda qd, cw_, c0_: lambda eng: eng.tensor_tensor(
                             qd[:, c0_:c0_ + cw_], stf[:, :cw_], mxf[:, :cw_],
                             Alu.mult))(qsl, cw, c0),
                         reads=("st", "mx"), writes=("q2",))
                p.dma("scalar", f"wqs{half}",
                      (lambda qs_, c: lambda eng: eng.dma_start(wq_hh[c], qs_))(qsl, ch),
                      reads=("q2",), writes=(scrname,))

        wquant(w1_h, wq1_h, DC, DF, sw1, "sw1", "wq1d")
        wquant(w2_h, wq2_h, FC, D, sw2, "sw2", "wq2d")

        x3 = x_sb[:, :, :]
        x13 = x1_sb[:, :, :]
        sq3 = sq_sb[:, :, :]
        st3 = st_sb[:, :, :]
        mx3 = mx_sb[:, :, :]
        ot3 = ot_sb[:, :, :]
        xi3 = xz_sb[:, 0:CC, :]
        sl3 = xz_sb[:, CC:2 * CC, :]
        bb3 = bb_sb[:, :, :]
        hh3 = hh_sb[:, :, :]

        def norm_stats(src_t, nch, srcbuf):
            nfeat = float(nch * 128)
            p.op("scalar",
                 (lambda s, n: lambda eng: eng.activation(
                     sq_sb[:, 0:n, :], s[:, 0:n, :], Act.Square))(src_t, nch),
                 reads=(srcbuf,), writes=("sq",))

            def sums(s=src_t, n=nch):
                def f(eng):
                    for c in range(n):
                        eng.matmul(prow[0:1, 0:TT], on_sb[:, 0:1], s[:, c, :],
                                   start=(c == 0), stop=False)
                    last = None
                    for c in range(n):
                        last = eng.matmul(prow[0:1, TT:2 * TT], on_sb[:, 0:1],
                                          sq_sb[:, c, :],
                                          start=False, stop=(c == n - 1))
                    return last
                return f
            p.op("tensor", sums(), reads=(srcbuf, "sq", "ones", "prow"),
                 writes=("prow",))
            p.op("vector",
                 (lambda nf: lambda eng: eng.tensor_scalar(
                     r_sm[:, :], prow[0:1, 0:TT], 1.0 / nf, None, Alu.mult))(nfeat),
                 reads=("prow",), writes=("rsm",))
            p.op("vector",
                 lambda eng: eng.tensor_mul(r_tmp[:, :], r_sm[:, :], r_sm[:, :]),
                 reads=("rsm",), writes=("rtmp",))
            p.op("vector",
                 (lambda nf: lambda eng: eng.scalar_tensor_tensor(
                     r_tmp[:, :], prow[0:1, TT:2 * TT], 1.0 / nf, r_tmp[:, :],
                     Alu.mult, Alu.subtract))(nfeat),
                 reads=("prow", "rtmp"), writes=("rtmp",))
            p.op("scalar",
                 lambda eng: eng.activation(r_rs[:, :], r_tmp[:, :], Act.Ln,
                                            bias=eps_sb[0:1, 0:1], scale=1.0),
                 reads=("rtmp", "eps"), writes=("rrs",))
            p.op("scalar",
                 lambda eng: eng.activation(r_rs[:, :], r_rs[:, :], Act.Exp,
                                            bias=0.0, scale=-0.5),
                 reads=("rrs",), writes=("rrs",))
            p.op("gpsimd",
                 lambda eng: eng.partition_broadcast(t_mm[:, :], r_sm[0:1, :], 128),
                 reads=("rsm",), writes=("mm",))
            p.op("gpsimd",
                 lambda eng: eng.partition_broadcast(t_rr[:, :], r_rs[0:1, :], 128),
                 reads=("rrs",), writes=("rr",))

        def x_load(kk):
            for b_ in range(2):
                p.dma("sync", f"x{b_}",
                      (lambda b2_, kt: lambda eng: eng.dma_start(
                          x_sb[:, :, b2_ * 128:(b2_ + 1) * 128], x_h[b2_, kt]))(b_, kk),
                      writes=("x",))

        def ln1_front(kk):
            norm_stats(x_sb, DC, "x")
            p.op("vector",
                 lambda eng: eng.tensor_tensor(sq3, x3, bc(t_mm, DC), Alu.subtract),
                 reads=("x", "mm"), writes=("sq",))
            p.op("vector",
                 lambda eng: eng.tensor_tensor(bb_sb[:, 0:DC, :], sq3, bc(t_rr, DC),
                                               Alu.mult),
                 reads=("sq", "rr"), writes=("bb",))

        def win_stage(kk):
            for jg in range(4):
                for k8 in range(DC):
                    li = jg * DC + k8
                    buf = winb[li % 2]
                    bn = f"winb{li % 2}"
                    p.dma("sync", bn,
                          (lambda bf, kc, j0: lambda eng: eng.dma_start(
                              bf[:, :], win_h[kc][:, j0 * 1024:(j0 + 1) * 1024]))(
                              buf, k8, jg),
                          writes=(bn,))

                    def mmg(bf=buf, k8_=k8):
                        def f(eng):
                            last = None
                            for j in range(8):
                                bank = pm[j // 2]
                                c0 = (j % 2) * TT
                                last = eng.matmul(
                                    bank[:, c0:c0 + TT],
                                    bf[:, j * 128:(j + 1) * 128],
                                    bb_sb[:, k8_, :],
                                    start=(k8_ == 0 and j % 2 == 0),
                                    stop=(k8_ == DC - 1 and j % 2 == 1))
                            return last
                        return f
                    p.op("tensor", mmg(),
                         reads=(bn, "bb", "pm0", "pm1", "pm2", "pm3"),
                         writes=("pm0", "pm1", "pm2", "pm3"))
                for bkn in range(4):
                    mglob = jg * 8 + 2 * bkn
                    if mglob < CC:
                        p.op("scalar",
                             (lambda bk, mg: lambda eng: eng.copy(
                                 xz_sb[:, mg:mg + 2, :], pm[bk][:, :]))(bkn, mglob),
                             reads=(f"pm{bkn}",), writes=("xz",))
                    else:
                        p.op("scalar",
                             (lambda bk, mg: lambda eng: eng.activation(
                                 xz_sb[:, mg:mg + 2, :], pm[bk][:, :],
                                 Act.Silu))(bkn, mglob),
                             reads=(f"pm{bkn}",), writes=("xz",))

        def mixer_mid(kk):
            def dmmg():
                def f(eng):
                    last = None
                    for k16 in range(CC):
                        last = eng.matmul(prow[0:1, 0:TT], wd_sb[:, k16:k16 + 1],
                                          xz_sb[:, k16, :],
                                          start=(k16 == 0), stop=(k16 == CC - 1))
                    return last
                return f
            p.op("tensor", dmmg(), reads=("wd", "xz", "prow"), writes=("prow",))

            def bcmmg():
                def f(eng):
                    last = None
                    for k16 in range(CC):
                        for hf in range(2):
                            last = eng.matmul(
                                pbc[:, hf * TT:(hf + 1) * TT],
                                wbc_sb[:, k16, hf * 64:(hf + 1) * 64],
                                xz_sb[:, k16, :],
                                start=(k16 == 0 and hf == 0),
                                stop=(k16 == CC - 1 and hf == 1))
                    return last
                return f
            p.op("tensor", bcmmg(), reads=("wbc", "xz", "pbc"), writes=("pbc",))
            p.op("scalar", lambda eng: eng.copy(braw[:, :], pbc[0:64, 0:TT]),
                 reads=("pbc",), writes=("braw",))
            p.op("scalar", lambda eng: eng.copy(craw[:, :], pbc[0:64, TT:2 * TT]),
                 reads=("pbc",), writes=("craw",))

            p.op("scalar",
                 lambda eng: eng.activation(r_ab[:, :], prow[0:1, 0:TT], Act.Abs),
                 reads=("prow",), writes=("rab",))
            p.op("scalar",
                 lambda eng: eng.activation(r_en[:, :], r_ab[:, :], Act.Exp,
                                            bias=0.0, scale=-1.0),
                 reads=("rab",), writes=("ren",))
            p.op("scalar",
                 lambda eng: eng.activation(r_ln[:, :], r_en[:, :], Act.Ln,
                                            bias=1.0, scale=1.0),
                 reads=("ren",), writes=("rln",))
            p.op("vector",
                 lambda eng: eng.tensor_scalar(r_mx[:, :], prow[0:1, 0:TT], 0.0,
                                               None, Alu.max),
                 reads=("prow",), writes=("rmx",))
            p.op("vector",
                 lambda eng: eng.tensor_add(r_dl[:, :], r_mx[:, :], r_ln[:, :]),
                 reads=("rmx", "rln"), writes=("rdl",))
            p.op("scalar",
                 lambda eng: eng.activation(r_ar[:, :], r_dl[:, :], Act.Exp,
                                            bias=0.0, scale=an_sb[0:1, 0:1]),
                 reads=("rdl", "aneg"), writes=("rar",))
            p.op("gpsimd",
                 lambda eng: eng.partition_broadcast(abc_sb[:, :], r_ar[0:1, :], 128),
                 reads=("rar",), writes=("abc",))

            for pr in range(8):
                cc0 = 2 * pr
                pair = slice(cc0, cc0 + 2)

                def bmm(c0=cc0):
                    def f(eng):
                        eng.matmul(pB[:, 0:TT], wB_sb[:, c0, :], braw[:, :],
                                   start=True, stop=False)
                        return eng.matmul(pB[:, TT:2 * TT], wB_sb[:, c0 + 1, :],
                                          braw[:, :], start=False, stop=True)
                    return f

                def cmm(c0=cc0):
                    def f(eng):
                        eng.matmul(pC[:, 0:TT], wC_sb[:, c0, :], craw[:, :],
                                   start=True, stop=False)
                        return eng.matmul(pC[:, TT:2 * TT], wC_sb[:, c0 + 1, :],
                                          craw[:, :], start=False, stop=True)
                    return f
                p.op("tensor", bmm(), reads=("wB", "braw", "pB"), writes=("pB",))
                p.op("tensor", cmm(), reads=("wC", "craw", "pC"), writes=("pC",))
                p.op("vector",
                     (lambda c0: lambda eng: eng.tensor_tensor(
                         bb_sb[:, c0:c0 + 2, :].rearrange("p c t -> p (c t)"),
                         pB[:, :],
                         xz_sb[:, c0:c0 + 2, :].rearrange("p c t -> p (c t)"),
                         Alu.mult))(cc0),
                     reads=("pB", "xz"), writes=("bb",))
                for ci in (cc0, cc0 + 1):
                    for sq_ in range(2):
                        t0 = sq_ * 128
                        if kk == 0:
                            init = 0.0
                        else:
                            init = carry[:, 2 * ci + sq_: 2 * ci + sq_ + 1]
                        p.op("vector",
                             (lambda c_, tt0, ini: lambda eng: eng.tensor_tensor_scan(
                                 hh_sb[:, c_, tt0:tt0 + 128],
                                 abc_sb[:, tt0:tt0 + 128],
                                 bb_sb[:, c_, tt0:tt0 + 128],
                                 ini, Alu.mult, Alu.add))(ci, t0, init),
                             reads=("abc", "bb", "carry"), writes=("hh",))
                p.op("vector",
                     (lambda c0: lambda eng: eng.tensor_copy(
                         carry[:, 2 * c0:2 * c0 + 4],
                         hh_sb[:, :, :].rearrange("p c t -> p (c t)")
                         [:, c0 * TT + 127::128][:, 0:4]))(cc0),
                     reads=("hh",), writes=("carry",))
                p.op("vector",
                     (lambda c0: lambda eng: eng.tensor_tensor(
                         bb_sb[:, c0:c0 + 2, :].rearrange("p c t -> p (c t)"),
                         pC[:, :],
                         hh_sb[:, c0:c0 + 2, :].rearrange("p c t -> p (c t)"),
                         Alu.mult))(cc0),
                     reads=("pC", "hh"), writes=("bb",))
                dD_b = dD_sb[:, pair].unsqueeze(2).broadcast_to([128, 2, TT])
                p.op("vector",
                     (lambda pa, db, pr_: lambda eng: eng.tensor_tensor(
                         sq_sb[:, 2 + (pr_ % 2) * 2:4 + (pr_ % 2) * 2, :],
                         xz_sb[:, pa, :], db, Alu.mult))(pair, dD_b, pr),
                     reads=("xz", "dD"), writes=("sq",))
                p.op("vector",
                     (lambda pa, pr_: lambda eng: eng.tensor_tensor(
                         sq_sb[:, 2 + (pr_ % 2) * 2:4 + (pr_ % 2) * 2, :],
                         bb_sb[:, pa, :],
                         sq_sb[:, 2 + (pr_ % 2) * 2:4 + (pr_ % 2) * 2, :],
                         Alu.add))(pair, pr),
                     reads=("bb", "sq"), writes=("sq",))
                p.op("vector",
                     (lambda pa, c0, pr_: lambda eng: eng.tensor_tensor(
                         hh_sb[:, pa, :],
                         sq_sb[:, 2 + (pr_ % 2) * 2:4 + (pr_ % 2) * 2, :],
                         xz_sb[:, CC + c0:CC + c0 + 2, :], Alu.mult))(pair, cc0, pr),
                     reads=("sq", "xz"), writes=("hh",))
                for mg in range(2):
                    for k16 in (cc0, cc0 + 1):
                        li = (pr * 4 + mg * 2 + (k16 - cc0)) % 2
                        buf = wob[li]
                        bn = f"wob{li}"
                        p.dma("sync", bn,
                              (lambda bf, kc, m0: lambda eng: eng.dma_start(
                                  bf[:, :], wo_h[kc][:, m0 * 512:(m0 + 1) * 512]))(
                                  buf, k16, mg),
                              writes=(bn,))

                        def womm(bf=buf, k16_=k16, mg_=mg):
                            def f(eng):
                                last = None
                                for j in range(4):
                                    bank = pm[2 * mg_ + j // 2]
                                    c0_ = (j % 2) * TT
                                    last = eng.matmul(
                                        bank[:, c0_:c0_ + TT],
                                        bf[:, j * 128:(j + 1) * 128],
                                        hh_sb[:, k16_, :],
                                        start=(k16_ == 0 and j % 2 == 0),
                                        stop=(k16_ == CC - 1 and j % 2 == 1))
                                return last
                            return f
                        p.op("tensor", womm(),
                             reads=(bn, "hh", f"pm{2 * mg}", f"pm{2 * mg + 1}"),
                             writes=(f"pm{2 * mg}", f"pm{2 * mg + 1}"))
            for bkn in range(4):
                mglob = 2 * bkn
                p.op("scalar",
                     (lambda b_, mg: lambda eng: eng.copy(
                         mx_sb[:, mg:mg + 2, :], pm[b_][:, :]))(bkn, mglob),
                     reads=(f"pm{bkn}",), writes=("mx",))

        def ln2_u(kk):
            norm_stats(mx_sb, DC, "mx")
            p.op("vector",
                 lambda eng: eng.tensor_tensor(sq3, mx3, bc(t_mm, DC), Alu.subtract),
                 reads=("mx", "mm"), writes=("sq",))
            p.op("vector",
                 lambda eng: eng.tensor_tensor(st3, sq3, bc(t_rr, DC), Alu.mult),
                 reads=("sq", "rr"), writes=("st",))
            p.op("vector",
                 lambda eng: eng.tensor_tensor(x13, st3, x3, Alu.add),
                 reads=("st", "x"), writes=("x1",))

        def ln3_q1(kk):
            norm_stats(x1_sb, DC, "x1")
            p.op("vector",
                 lambda eng: eng.tensor_tensor(sq3, x13, bc(t_mm, DC), Alu.subtract),
                 reads=("x1", "mm"), writes=("sq",))
            p.op("vector",
                 lambda eng: eng.tensor_tensor(mx3, sq3, bc(t_rr, DC), Alu.mult),
                 reads=("sq", "rr"), writes=("mx",))
            p.op("scalar",
                 lambda eng: eng.activation(sq3, mx3, Act.Square),
                 reads=("mx",), writes=("sq",))

            def sq_sums():
                def f(eng):
                    last = None
                    for c in range(DC):
                        last = eng.matmul(prow[0:1, 0:TT], on_sb[:, 0:1],
                                          sq_sb[:, c, :],
                                          start=(c == 0), stop=(c == DC - 1))
                    return last
                return f
            p.op("tensor", sq_sums(), reads=("sq", "ones", "prow"),
                 writes=("prow",))
            p.op("vector",
                 lambda eng: eng.tensor_reduce(
                     t_ms[:, :], sq3.transpose([0, 2, 1]), AX.X, Alu.max),
                 reads=("sq",), writes=("tms",))
            p.op("gpsimd",
                 lambda eng: eng.partition_all_reduce(
                     t_ax[:, :], t_ms[:, :], 128, ReduceOp.max),
                 reads=("tms",), writes=("ax",))
            p.op("scalar",
                 lambda eng: eng.activation(r_rs[:, :], prow[0:1, 0:TT], Act.Ln,
                                            bias=eps_sb[0:1, 0:1], scale=1.0 / D),
                 reads=("prow", "eps"), writes=("rrs",))
            p.op("scalar",
                 lambda eng: eng.activation(r_rs[:, :], r_rs[:, :], Act.Exp,
                                            bias=0.0, scale=-0.5),
                 reads=("rrs",), writes=("rrs",))
            p.op("gpsimd",
                 lambda eng: eng.partition_broadcast(t_rr[:, :], r_rs[0:1, :], 128),
                 reads=("rrs",), writes=("rr",))
            p.op("scalar",
                 lambda eng: eng.activation(t_ms[:, :], t_ax[:, :], Act.Ln),
                 reads=("ax",), writes=("tms",))
            p.op("scalar",
                 lambda eng: eng.activation(t_ax[:, :], t_ms[:, :], Act.Exp,
                                            bias=0.0, scale=0.5),
                 reads=("tms",), writes=("ax",))
            p.op("vector",
                 lambda eng: eng.tensor_mul(t_ax[:, :], t_ax[:, :], t_rr[:, :]),
                 reads=("ax", "rr"), writes=("ax",))
            p.op("vector",
                 lambda eng: eng.tensor_scalar(t_ax[:, :], t_ax[:, :], 1e-5, None,
                                               Alu.max),
                 reads=("ax",), writes=("ax",))
            p.op("vector", lambda eng: eng.reciprocal(t_rc[:, :], t_ax[:, :]),
                 reads=("ax",), writes=("rc",))
            p.op("vector",
                 lambda eng: eng.scalar_tensor_tensor(
                     t_sc[:, :], t_rr[:, :], 127.0, t_rc[:, :], Alu.mult, Alu.mult),
                 reads=("rr", "rc"), writes=("sc",))
            p.op("vector",
                 lambda eng: eng.tensor_scalar(t_fs[:, :], t_ax[:, :], m1c[:, 0:1],
                                               1.0 / 127.0, Alu.mult, Alu.mult),
                 reads=("ax", "m1c"), writes=("fs",))
            p.op("vector",
                 lambda eng: eng.tensor_tensor(hh_sb[:, 0:DC, :], mx3, bc(t_sc, DC),
                                               Alu.mult),
                 reads=("mx", "sc"), writes=("hh",))
            p.op("scalar",
                 lambda eng: eng.activation(sq3, hh_sb[:, 0:DC, :], Act.Copy,
                                            bias=C32, scale=1.0),
                 reads=("hh",), writes=("sq",))
            p.op("scalar",
                 lambda eng: eng.activation(st3, hh_sb[:, 0:DC, :], Act.Copy,
                                            bias=C64, scale=1.0),
                 reads=("hh",), writes=("st",))
            p.op("vector",
                 lambda eng: eng.scalar_tensor_tensor(
                     st3, st3, -C64, hh_sb[:, 0:DC, :], Alu.add, Alu.subtract),
                 reads=("st", "hh"), writes=("st",))
            p.op("vector",
                 lambda eng: eng.scalar_tensor_tensor(
                     st3, st3, fF[:, 0:1], sq3, Alu.mult, Alu.add),
                 reads=("st", "fF", "sq"), writes=("st",))
            p.op("vector",
                 lambda eng: eng.tensor_scalar(q1_sb[:, :, :], st3, -C32, None,
                                               Alu.add),
                 reads=("st",), writes=("q1",))

        def ffn1_stage(kk):
            for jg in range(4):
                for k8 in range(DC):
                    li = jg * DC + k8
                    buf = wq1b[li % 2]
                    bn = f"wq1b{li % 2}"
                    p.dma("sync", bn,
                          (lambda bf, kc, j0: lambda eng: eng.dma_start(
                              bf[:, :], wq1_h[kc][:, j0 * 1024:(j0 + 1) * 1024]))(
                              buf, k8, jg),
                          reads=("wq1d",), writes=(bn,))

                    def f1mm(bf=buf, k8_=k8):
                        def f(eng):
                            last = None
                            for j in range(8):
                                bank = pm[j // 2]
                                c0 = (j % 2) * TT
                                last = eng.matmul(
                                    bank[:, c0:c0 + TT],
                                    bf[:, j * 128:(j + 1) * 128],
                                    q1_sb[:, k8_, :],
                                    start=(k8_ == 0 and j % 2 == 0),
                                    stop=(k8_ == DC - 1 and j % 2 == 1))
                            return last
                        return f
                    p.op("tensor", f1mm(),
                         reads=(bn, "q1", "pm0", "pm1", "pm2", "pm3"),
                         writes=("pm0", "pm1", "pm2", "pm3"))
                for bkn in range(4):
                    mglob = jg * 8 + 2 * bkn
                    fs_b1 = t_fs[:, :].unsqueeze(1).broadcast_to([128, 1, TT])
                    for half in range(2):
                        mg1 = mglob + half
                        slot = sq_sb[:, 2 * (bkn % 2) + half, :]
                        p.op("vector",
                             (lambda bk, sl, hf: lambda eng:
                              eng.scalar_tensor_tensor(
                                  sl, pm[bk][:, hf * TT:(hf + 1) * TT], 0.0,
                                  t_fs[:, :], Alu.bypass, Alu.mult))(
                                  bkn, slot, half),
                             reads=(f"pm{bkn}", "fs"), writes=("sq",))
                        p.op("scalar",
                             (lambda sl, mg: lambda eng: eng.activation(
                                 xz_sb[:, mg, :], sl, Act.Gelu,
                                 bias=b1_sb[:, mg:mg + 1], scale=1.0))(slot, mg1),
                             reads=("sq", "b1c"), writes=("xz",))

        def stats2_q2(kk):
            for pc in range(4):
                gsl = xz_sb[:, pc * 8:(pc + 1) * 8, :]
                p.op("scalar",
                     (lambda gs: lambda eng: eng.activation(
                         sq3, gs, Act.Square))(gsl),
                     reads=("xz",), writes=("sq",))

                def sq_sums2(pc_=pc):
                    def f(eng):
                        last = None
                        for c in range(DC):
                            last = eng.matmul(
                                prow[0:1, 0:TT], on_sb[:, 0:1], sq_sb[:, c, :],
                                start=(pc_ == 0 and c == 0),
                                stop=(pc_ == 3 and c == DC - 1))
                        return last
                    return f
                p.op("tensor", sq_sums2(), reads=("sq", "ones", "prow"),
                     writes=("prow",))
                p.op("vector",
                     lambda eng: eng.tensor_reduce(
                         t_ms[:, :], sq3.transpose([0, 2, 1]), AX.X, Alu.max),
                     reads=("sq",), writes=("tms",))
                if pc == 0:
                    p.op("vector",
                         lambda eng: eng.tensor_copy(t_am[:, :], t_ms[:, :]),
                         reads=("tms",), writes=("am",))
                else:
                    p.op("vector",
                         lambda eng: eng.tensor_tensor(t_am[:, :], t_am[:, :],
                                                       t_ms[:, :], Alu.max),
                         reads=("tms", "am"), writes=("am",))
            p.op("gpsimd",
                 lambda eng: eng.partition_all_reduce(
                     t_ax[:, :], t_am[:, :], 128, ReduceOp.max),
                 reads=("am",), writes=("ax",))
            p.op("scalar",
                 lambda eng: eng.activation(r_rs[:, :], prow[0:1, 0:TT], Act.Ln,
                                            bias=eps_sb[0:1, 0:1], scale=1.0 / DF),
                 reads=("prow", "eps"), writes=("rrs",))
            p.op("scalar",
                 lambda eng: eng.activation(r_rs[:, :], r_rs[:, :], Act.Exp,
                                            bias=0.0, scale=-0.5),
                 reads=("rrs",), writes=("rrs",))
            p.op("gpsimd",
                 lambda eng: eng.partition_broadcast(t_rr[:, :], r_rs[0:1, :], 128),
                 reads=("rrs",), writes=("rr",))
            p.op("scalar",
                 lambda eng: eng.activation(t_ms[:, :], t_ax[:, :], Act.Ln),
                 reads=("ax",), writes=("tms",))
            p.op("scalar",
                 lambda eng: eng.activation(t_ax[:, :], t_ms[:, :], Act.Exp,
                                            bias=0.0, scale=0.5),
                 reads=("tms",), writes=("ax",))
            p.op("vector",
                 lambda eng: eng.tensor_mul(t_ax[:, :], t_ax[:, :], t_rr[:, :]),
                 reads=("ax", "rr"), writes=("ax",))
            p.op("vector",
                 lambda eng: eng.tensor_scalar(t_ax[:, :], t_ax[:, :], 1e-5, None,
                                               Alu.max),
                 reads=("ax",), writes=("ax",))
            p.op("vector", lambda eng: eng.reciprocal(t_rc[:, :], t_ax[:, :]),
                 reads=("ax",), writes=("rc",))
            p.op("vector",
                 lambda eng: eng.scalar_tensor_tensor(
                     t_sc[:, :], t_rr[:, :], 127.0, t_rc[:, :], Alu.mult, Alu.mult),
                 reads=("rr", "rc"), writes=("sc",))
            p.op("vector",
                 lambda eng: eng.tensor_scalar(t_fs[:, :], t_ax[:, :], m2c[:, 0:1],
                                               1.0 / 127.0, Alu.mult, Alu.mult),
                 reads=("ax", "m2c"), writes=("fs",))
        def q2_ffn2(kk):
            for pc in range(4):
                gsl = xz_sb[:, pc * 8:(pc + 1) * 8, :]
                p.op("vector",
                     (lambda gs: lambda eng: eng.tensor_tensor(
                         mx3, gs, bc(t_sc, DC), Alu.mult))(gsl),
                     reads=("xz", "sc"), writes=("mx",))
                p.op("scalar",
                     lambda eng: eng.activation(sq3, mx3, Act.Copy,
                                                bias=C32, scale=1.0),
                     reads=("mx",), writes=("sq",))
                p.op("scalar",
                     lambda eng: eng.activation(st3, mx3, Act.Copy,
                                                bias=C64, scale=1.0),
                     reads=("mx",), writes=("st",))
                p.op("vector",
                     lambda eng: eng.scalar_tensor_tensor(
                         st3, st3, -C64, mx3, Alu.add, Alu.subtract),
                     reads=("st", "mx"), writes=("st",))
                p.op("vector",
                     lambda eng: eng.scalar_tensor_tensor(
                         st3, st3, fF[:, 0:1], sq3, Alu.mult, Alu.add),
                     reads=("st", "fF", "sq"), writes=("st",))
                p.op("vector",
                     (lambda p_: lambda eng: eng.tensor_scalar(
                         q2_sb[:, p_ * 8:(p_ + 1) * 8, :], st3, -C32, None,
                         Alu.add))(pc),
                     reads=("st",), writes=("q2",))
                for k32 in range(pc * 8, (pc + 1) * 8):
                    buf = wq2b[k32 % 2]
                    bn = f"wq2b{k32 % 2}"
                    p.dma("sync", bn,
                          (lambda bf, kc: lambda eng: eng.dma_start(
                              bf[:, :], wq2_h[kc]))(buf, k32),
                          reads=("wq2d",), writes=(bn,))

                    def f2mm(bf=buf, k32_=k32):
                        def f(eng):
                            last = None
                            for j in range(8):
                                bank = pm[j // 2]
                                c0 = (j % 2) * TT
                                last = eng.matmul(
                                    bank[:, c0:c0 + TT],
                                    bf[:, j * 128:(j + 1) * 128],
                                    q2_sb[:, k32_, :],
                                    start=(k32_ == 0 and j % 2 == 0),
                                    stop=(k32_ == FC - 1 and j % 2 == 1))
                            return last
                        return f
                    p.op("tensor", f2mm(),
                         reads=(bn, "q2", "pm0", "pm1", "pm2", "pm3"),
                         writes=("pm0", "pm1", "pm2", "pm3"))

        def ffn2_evac(kk):
            for bkn in range(4):
                mglob = 2 * bkn
                fs_b = t_fs[:, :].unsqueeze(1).broadcast_to([128, 2, TT])
                p.op("vector",
                     (lambda bk, mg: lambda eng: eng.scalar_tensor_tensor(
                         mx_sb[:, mg:mg + 2, :],
                         pm[bk][:, :].rearrange("p (c t) -> p c t", c=2), 0.0,
                         fs_b,
                         Alu.bypass, Alu.mult))(bkn, mglob),
                     reads=(f"pm{bkn}", "fs"), writes=("mx",))
            b2_b = b2_sb[:, :].unsqueeze(2).broadcast_to([128, DC, TT])
            p.op("vector",
                 lambda eng: eng.tensor_tensor(mx3, mx3, b2_b, Alu.add),
                 reads=("mx", "b2c"), writes=("mx",))

        def ln4_out(kk):
            norm_stats(mx_sb, DC, "mx")
            p.op("vector",
                 lambda eng: eng.tensor_tensor(sq3, mx3, bc(t_mm, DC), Alu.subtract),
                 reads=("mx", "mm"), writes=("sq",))
            p.op("vector",
                 lambda eng: eng.tensor_tensor(st3, sq3, bc(t_rr, DC), Alu.mult),
                 reads=("sq", "rr"), writes=("st",))
            p.op("vector",
                 lambda eng: eng.tensor_tensor(ot3, st3, x13, Alu.add),
                 reads=("st", "x1"), writes=("ot",))
            for b_ in range(2):
                p.dma("scalar", f"out{b_}",
                      (lambda b2_, kt: lambda eng: eng.dma_start(
                          o_h[b2_, kt], ot_sb[:, :, b2_ * 128:(b2_ + 1) * 128]))(b_, kk),
                      reads=("ot",), writes=())

        x_load(0)
        ln1_front(0)
        win_stage(0)
        for k in range(n_tiles):
            mixer_mid(k)
            ln2_u(k)
            if k + 1 < n_tiles:
                x_load(k + 1)
            ln3_q1(k)
            if k + 1 < n_tiles:
                ln1_front(k + 1)
            ffn1_stage(k)
            stats2_q2(k)
            q2_ffn2(k)
            ffn2_evac(k)
            if k + 1 < n_tiles:
                win_stage(k + 1)
            ln4_out(k)

        p.final_wait("sync", "out0")
        p.final_wait("sync", "out1")

        _emit(nc, p, esems, dsems)

    m = {}
    for lib in all_libraries:
        for t in lib.instructions:
            m[t] = m.get(t, 0) | (1 << lib.index)
    _bass_rust.insert_library_loads(nc, m, len(all_libraries), standard.index)
    _bass_rust.insert_act_table_loads(nc)
    mybir.codegen_inst_isa_subclasses(nc)
    return nc


def make_in_map(core_id, inputs):
    x = np.asarray(inputs["L0_attn_x"])
    W_in = np.ascontiguousarray(np.asarray(inputs["L0_attn_W_in"], np.float32))
    W_xp = np.asarray(inputs["L0_attn_W_xp"])
    W_B = np.asarray(inputs["L0_attn_W_B"])
    W_C = np.asarray(inputs["L0_attn_W_C"])
    W_out = np.asarray(inputs["L0_attn_W_out"])
    Dv = np.asarray(inputs["L0_attn_D"])
    A_log = np.asarray(inputs["L0_attn_A_log"])
    w1 = np.asarray(inputs["L0_ffn_w1"])
    b1 = np.asarray(inputs["L0_ffn_b1"])
    w2 = np.asarray(inputs["L0_ffn_w2"])
    b2 = np.asarray(inputs["L0_ffn_b2"])

    xc = x[2 * core_id:2 * core_id + 2]
    xt = xc.transpose(0, 2, 1).reshape(2, DC, 128, NT, 128)
    xt = np.ascontiguousarray(xt.transpose(0, 3, 2, 1, 4)).astype(np.float32)

    return {
        "x": xt,
        "w_in": np.ascontiguousarray(W_in.reshape(DC, 128, 2 * DI), np.float32),
        "w_xd": np.ascontiguousarray(
            W_xp[:, 0].reshape(CC, 128).T, np.float32),
        "w_xbc": np.ascontiguousarray(
            W_xp[:, 1:129].reshape(CC, 128, 128).transpose(1, 0, 2), np.float32),
        "w_B": np.ascontiguousarray(
            W_B.reshape(64, CC, 128), np.float32),
        "w_C": np.ascontiguousarray(
            W_C.reshape(64, CC, 128), np.float32),
        "w_out": np.ascontiguousarray(W_out.reshape(CC, 128, D), np.float32),
        "d_vec": np.ascontiguousarray(Dv.reshape(CC, 128).T, np.float32),
        "a_log": np.asarray(A_log, np.float32).reshape(1, 1),
        "w1t": np.ascontiguousarray(w1.T.reshape(DC, 128, DF), np.float32),
        "b1c": np.ascontiguousarray(b1.reshape(FC, 128).T, np.float32),
        "w2t": np.ascontiguousarray(w2.T.reshape(FC, 128, D), np.float32),
        "b2c": np.ascontiguousarray(b2.reshape(DC, 128).T, np.float32),
    }


def assemble_out(res_list):
    outs = []
    for r in res_list:
        o = np.asarray(r["out"])
        o = o.transpose(0, 3, 2, 1, 4).reshape(2, D, 2048).transpose(0, 2, 1)
        outs.append(o)
    return np.ascontiguousarray(np.concatenate(outs, 0), dtype=np.float32)


def kernel(**inputs):
    nc = build_program()
    in_maps = [make_in_map(c, inputs) for c in range(8)]
    res = run_bass_kernel_spmd(nc, in_maps, core_ids=list(range(8)))
    return assemble_out(res.results)
